# version 03
# baseline (speedup 1.0000x reference)
import numpy as np
import ml_dtypes
from contextlib import ExitStack

import concourse.bass as bass
import concourse.mybir as mybir
from concourse.bass_utils import run_bass_kernel_spmd

F32 = mybir.dt.float32
BF16 = mybir.dt.bfloat16
I32 = mybir.dt.int32
AF = mybir.ActivationFunctionType
ALU = mybir.AluOpType
AX = mybir.AxisListType

NCORE = 8
B = 2
S = 8192
D = 2048
T = 2048
NT = 16
ND = 16
PT = 64
EPS_LN = 1e-5

HA = 16; HKV = 8; DH = 128
HB = 16; D_NOPE = 128; D_ROPE = 64; D_V = 128; R_Q = 1536; R_KV = 512
ATTN_SCALE = 1.0 / float(np.sqrt(192.0))
NE = 32; INTER = 768; TOPK = 8; TAU = 0.05

BF = ml_dtypes.bfloat16
INV2PI = 1.0 / (2.0 * np.pi)
TWOPI = 2.0 * np.pi


def owner_tiles(c):
    return [c + 8 * k for k in range(8)]



class Tok:
    __slots__ = ("sem", "val")
    def __init__(self, sem, val):
        self.sem = sem
        self.val = val


ENGS = ("PE", "ACT", "DVE", "POOL", "SP")


class Prog:
    def __init__(self, nc, stack, n_spd=10, n_swd=6):
        self.nc = nc
        self.selfsem = {e: stack.enter_context(nc.semaphore(f"S{e}")) for e in ENGS}
        self.count = {e: 0 for e in ENGS}
        self.clock = {e: {} for e in ENGS}
        self.writer = {}
        self.readers = {}
        self.spd = [stack.enter_context(nc.semaphore(f"spd{i}")) for i in range(n_spd)]
        self.spd_count = [0] * n_spd
        self.spd_next = 0
        self.swd = [stack.enter_context(nc.semaphore(f"swd{i}")) for i in range(n_swd)]
        self.swd_count = [0] * n_swd
        self.swd_next = 0
        self.ccsem = stack.enter_context(nc.semaphore("cc"))
        self.cc_count = 0
        self.blocks = [{e: [] for e in ENGS}]
        self.n_inst = 0
        self.dynst = {}

    def emit_off(self, e, q, const, cp=0, cq=0, cm=0):
        st = self.dynst.get(q)
        if st is None:
            pid = e.partition_id()
            r_pid = e.to_reg(pid)
            r_q = e.alloc_register(f"{q}_dq")
            r_m = e.alloc_register(f"{q}_dm")
            r_off = e.alloc_register(f"{q}_do")
            r_t = e.alloc_register(f"{q}_dt")
            e.reg_div(r_q, r_pid, 4)
            e.reg_mod(r_m, r_pid, 4)
            st = {"pid": r_pid, "q": r_q, "m": r_m, "off": r_off, "t": r_t}
            self.dynst[q] = st
        e.reg_mov(st["off"], const)
        for coef, rsrc in ((cp, st["pid"]), (cq, st["q"]), (cm, st["m"])):
            if coef:
                e.reg_mul(st["t"], rsrc, coef)
                e.reg_add(st["off"], st["off"], st["t"])
        return st["off"]

    def _deps(self, reads, writes):
        toks = []
        for r in reads:
            w = self.writer.get(r)
            if w is not None:
                toks.append(w)
        for w in writes:
            ww = self.writer.get(w)
            if ww is not None:
                toks.append(ww)
            toks.extend(self.readers.get(w, ()))
        return toks

    def _waits_for(self, eng, toks, pe_exempt=False):
        m = {}
        for t in toks:
            if pe_exempt and eng == "PE" and t.sem is self.selfsem["PE"]:
                continue
            k = t.sem
            if m.get(k, 0) < t.val:
                m[k] = t.val
        out = []
        clock = self.clock[eng]
        for k, v in m.items():
            if clock.get(k, 0) < v:
                out.append((k, v))
                clock[k] = v
        return out

    def _register(self, tok, reads, writes):
        for r in reads:
            self.readers.setdefault(r, []).append(tok)
        for w in writes:
            self.writer[w] = tok
            self.readers[w] = []

    def op(self, eng, fn, reads=(), writes=()):
        waits = self._waits_for(eng, self._deps(reads, writes), pe_exempt=True)
        self.count[eng] += 1
        tok = Tok(self.selfsem[eng], self.count[eng])
        self.blocks[-1][eng].append((waits, fn, [(tok.sem, 1)]))
        self._register(tok, reads, writes)
        self.n_inst += 1
        return tok

    def group(self, eng, fns, reads=(), writes=()):
        waits = self._waits_for(eng, self._deps(reads, writes), pe_exempt=True)
        self.count[eng] += 1
        tok = Tok(self.selfsem[eng], self.count[eng])
        items = self.blocks[-1][eng]
        for i, fn in enumerate(fns):
            if i == 0:
                items.append((waits, fn, [] if len(fns) > 1 else [(tok.sem, 1)]))
            elif i == len(fns) - 1:
                items.append(([], fn, [(tok.sem, 1)]))
            else:
                items.append(([], fn, []))
        self._register(tok, reads, writes)
        self.n_inst += len(fns)
        return tok

    def dma(self, out_ap, in_ap, reads=(), writes=(), eng="SP", **kw):
        deps = self._deps(reads, writes)
        if eng == "SP":
            q, sems, counts = "SP", self.spd, self.spd_count
            i = self.spd_next
            self.spd_next = (self.spd_next + 1) % len(sems)
        else:
            q, sems, counts = "POOL", self.swd, self.swd_count
            i = self.swd_next
            self.swd_next = (self.swd_next + 1) % len(sems)
        sem = sems[i]
        if counts[i] > 0:
            deps = list(deps) + [Tok(sem, counts[i])]
        waits = self._waits_for(q, deps)
        counts[i] += 16

        def fn(e, out_ap=out_ap, in_ap=in_ap, kw=kw):
            o = out_ap(e) if callable(out_ap) else out_ap
            ii = in_ap(e) if callable(in_ap) else in_ap
            return e.dma_start(o, ii, **kw)
        tok = Tok(sem, counts[i])
        self.blocks[-1][q].append((waits, fn, [(sem, 16)]))
        self._register(tok, reads, writes)
        self.n_inst += 1
        return tok

    def collective(self, kind, op, ins, outs, reads=(), writes=()):
        waits = self._waits_for("POOL", self._deps(reads, writes))
        self.cc_count += 1
        tok = Tok(self.ccsem, self.cc_count)
        rg = [list(range(NCORE))]
        self.blocks[-1]["POOL"].append(
            (waits,
             lambda e: e.collective_compute(kind, op, replica_groups=rg,
                                            ins=[a.opt() for a in ins],
                                            outs=[a.opt() for a in outs]),
             [(self.ccsem, 1)]))
        self._register(tok, reads, writes)
        self.n_inst += 1
        return tok

    def barrier(self):
        toks = [Tok(self.selfsem[e], self.count[e]) for e in ENGS if self.count[e]]
        toks += [Tok(self.spd[i], self.spd_count[i])
                 for i in range(len(self.spd)) if self.spd_count[i]]
        toks += [Tok(self.swd[i], self.swd_count[i])
                 for i in range(len(self.swd)) if self.swd_count[i]]
        if self.cc_count:
            toks.append(Tok(self.ccsem, self.cc_count))
        for e in ENGS:
            waits = self._waits_for(e, toks)
            if waits:
                self.blocks[-1][e].append((waits, None, []))
        self.blocks.append({e: [] for e in ENGS})

    def drain_dmas(self, eng="SP"):
        toks = [Tok(self.spd[i], self.spd_count[i])
                for i in range(len(self.spd)) if self.spd_count[i]]
        toks += [Tok(self.swd[i], self.swd_count[i])
                 for i in range(len(self.swd)) if self.swd_count[i]]
        if self.cc_count:
            toks.append(Tok(self.ccsem, self.cc_count))
        waits = self._waits_for(eng, toks)
        if waits:
            self.blocks[-1][eng].append((waits, None, []))

    def finalize(self):
        nc = self.nc
        attr = {"PE": "tensor", "ACT": "scalar", "DVE": "vector",
                "POOL": "gpsimd", "SP": "sync"}
        for blk in self.blocks:
            if not any(blk[e] for e in ENGS):
                continue
            with nc.Block() as b:
                for e in ENGS:
                    items = blk[e]
                    if not items:
                        continue
                    def make(items):
                        def body(eng):
                            for waits, fn, incs in items:
                                for (sem, v) in waits:
                                    eng.wait_ge(sem, v)
                                if fn is None:
                                    continue
                                inst = fn(eng)
                                for (sem, n) in incs:
                                    inst.then_inc(sem, n)
                        return body
                    getattr(b, attr[e])(make(items))



def rope_tables():
    inv = 1.0 / (10000.0 ** (np.arange(0, DH, 2, dtype=np.float64) / DH))
    pos = np.arange(S, dtype=np.float64)
    fr = pos[:, None] * inv[None, :]
    emb = np.concatenate([fr, fr], axis=1)
    return (np.cos(emb).astype(np.float32).T.copy(),
            np.sin(emb).astype(np.float32).T.copy())


def perm_rope_cols(n_heads):
    cols = []
    for h in range(n_heads):
        cols.extend([h * 64 + 2 * k for k in range(32)])
        cols.extend([h * 64 + 2 * k + 1 for k in range(32)])
    return np.array(cols)


def core_offsets(c):
    off = np.zeros(64, dtype=np.int64)
    for ltp in range(16):
        b, k = ltp // 8, ltp % 8
        h = 2 * k + c // 4
        off[ltp] = (h * 128 + (2 * (c % 4) + 8 * b) * 8) * D
    off[16] = c * 128
    off[17] = c
    return off.astype(np.int32)


def host_inputs(inp):
    f32 = np.float32
    x = np.ascontiguousarray(inp["L0_attn_a_hidden_states"], dtype=f32)
    x4 = x.reshape(B, PT, 128, D)
    cos_t, sin_t = rope_tables()

    shared = {}
    shared["aWqT"] = np.ascontiguousarray(inp["L0_attn_a_Wq"].T, f32)
    shared["aWkT"] = np.ascontiguousarray(inp["L0_attn_a_Wk"].T, f32)
    shared["aWvT"] = np.ascontiguousarray(inp["L0_attn_a_Wv"].T, f32)
    shared["aWoT"] = np.ascontiguousarray(inp["L0_attn_a_Wo"].T, f32)
    shared["abq"] = np.ascontiguousarray(inp["L0_attn_a_bq"], f32)
    shared["abk"] = np.ascontiguousarray(inp["L0_attn_a_bk"], f32)
    shared["abv"] = np.ascontiguousarray(inp["L0_attn_a_bv"], f32)
    shared["abo"] = np.ascontiguousarray(inp["L0_attn_a_bo"], f32)
    shared["aRT"] = np.ascontiguousarray(inp["L0_attn_a_rand_matrix"].T, f32)
    shared["bWcq"] = np.ascontiguousarray(inp["L0_attn_b_w_cq"], f32)
    shared["bqnw"] = np.ascontiguousarray(inp["L0_attn_b_q_norm_w"], f32)
    shared["bWdqn"] = np.ascontiguousarray(inp["L0_attn_b_w_dq_nope"], f32)
    shared["bWdqr"] = np.ascontiguousarray(inp["L0_attn_b_w_dq_rope"][:, perm_rope_cols(HB)], f32)
    shared["bWckv"] = np.ascontiguousarray(inp["L0_attn_b_w_ckv"], f32)
    shared["bkvnw"] = np.ascontiguousarray(inp["L0_attn_b_kv_norm_w"], f32)
    shared["bWdkn"] = np.ascontiguousarray(inp["L0_attn_b_w_dk_nope"], f32)
    shared["bWdv"] = np.ascontiguousarray(inp["L0_attn_b_w_dv"], f32)
    shared["bWkr"] = np.ascontiguousarray(inp["L0_attn_b_w_k_rope"][:, perm_rope_cols(1)], f32)
    shared["bWproj"] = np.ascontiguousarray(inp["L0_attn_b_w_proj"], f32)
    shared["bfc"] = np.ascontiguousarray(inp["L0_attn_b_freqs_cos"], f32)
    shared["bfs"] = np.ascontiguousarray(inp["L0_attn_b_freqs_sin"], f32)
    shared["mWgT"] = np.ascontiguousarray(inp["L0_ffn_gate_weight"].T, f32)
    shared["mWguT"] = np.ascontiguousarray(inp["L0_ffn_gate_up_proj"].transpose(0, 2, 1), f32)
    shared["mWdT"] = np.ascontiguousarray(inp["L0_ffn_down_proj"].transpose(0, 2, 1), f32)
    shared["malpha"] = np.ascontiguousarray(np.asarray(inp["L0_ffn_layer_alpha"]).reshape(1), f32)
    shared["identF"] = np.eye(128, dtype=f32)
    shared["identB"] = np.eye(128).astype(BF)
    shared["tril"] = np.tril(np.ones((128, 128))).astype(BF)
    oh = np.zeros((NE, 32, 128), dtype=f32)
    for e in range(NE):
        oh[e, e, :] = 1.0
    shared["onehot"] = oh.astype(BF)

    in_maps = []
    for c in range(NCORE):
        tiles = owner_tiles(c)
        xs = np.concatenate([x4[b, p] for b in range(B) for p in tiles], axis=0)
        cs = np.concatenate([cos_t[:, p * 128:(p + 1) * 128] for b in range(B) for p in tiles], axis=1)
        ss = np.concatenate([sin_t[:, p * 128:(p + 1) * 128] for b in range(B) for p in tiles], axis=1)
        m = dict(shared)
        m["x"] = np.ascontiguousarray(xs)
        m["ropec"] = np.ascontiguousarray(cs)
        m["ropes"] = np.ascontiguousarray(ss)
        m["coff"] = core_offsets(c).reshape(1, 64)
        in_maps.append(m)
    return in_maps



def build_nc(debug_dump=None):
    nc = bass.Bass("TRN2", target_bir_lowering=False)

    I = {}
    def din(name, shape, dtype=F32):
        I[name] = nc.dram_tensor(name, list(shape), dtype, kind="ExternalInput").ap()
    din("x", (T, D))
    din("ropec", (128, T)); din("ropes", (128, T))
    din("aWqT", (D, D)); din("aWkT", (D, 1024)); din("aWvT", (D, 1024)); din("aWoT", (D, D))
    din("abq", (D,)); din("abk", (1024,)); din("abv", (1024,)); din("abo", (D,))
    din("aRT", (128, 64))
    din("bWcq", (D, R_Q)); din("bqnw", (R_Q,))
    din("bWdqn", (R_Q, D)); din("bWdqr", (R_Q, 1024))
    din("bWckv", (D, R_KV)); din("bkvnw", (R_KV,))
    din("bWdkn", (R_KV, D)); din("bWdv", (R_KV, D)); din("bWkr", (D, 64))
    din("bWproj", (D, D)); din("bfc", (HB, 32)); din("bfs", (HB, 32))
    din("mWgT", (D, NE)); din("mWguT", (NE, D, 2 * INTER)); din("mWdT", (NE, INTER, D))
    din("malpha", (1,))
    din("identF", (128, 128)); din("identB", (128, 128), BF16); din("tril", (128, 128), BF16)
    din("coff", (1, 64), I32)
    din("onehot", (NE, 32, 128), BF16)

    out = nc.dram_tensor("out_shard", [T, D], F32, kind="ExternalOutput").ap()
    dbg = None
    if debug_dump is not None:
        dbg = nc.dram_tensor("dbg", list(debug_dump), F32, kind="ExternalOutput").ap()

    xr1 = nc.dram_tensor("xr1", [T, D], F32, kind="Internal").ap()
    xr2 = nc.dram_tensor("xr2", [T, D], F32, kind="Internal").ap()
    q_shard = nc.dram_tensor("q_shard", [HB * 192, T], BF16, kind="Internal").ap()
    qag_in = nc.dram_tensor("qag_in", [HB * 192, T], BF16, kind="Internal").ap()
    qag_out = nc.dram_tensor("qag_out", [NCORE * HB * 192, T], BF16, kind="Internal").ap()
    kag_in = nc.dram_tensor("kag_in", [HB * 192, T], BF16, kind="Internal").ap()
    kag_out = nc.dram_tensor("kag_out", [NCORE * HB * 192, T], BF16, kind="Internal").ap()
    vag_in = nc.dram_tensor("vag_in", [T, D], BF16, kind="Internal").ap()
    vag_out = nc.dram_tensor("vag_out", [NCORE * T, D], BF16, kind="Internal").ap()
    oag_in = nc.dram_tensor("oag_in", [4 * 128, S], BF16, kind="Internal").ap()
    oag_out = nc.dram_tensor("oag_out", [NCORE * 4 * 128, S], BF16, kind="Internal").ap()
    qloc = nc.dram_tensor("qloc", [192, S], BF16, kind="Internal").ap()
    kloc = nc.dram_tensor("kloc", [192, S], BF16, kind="Internal").ap()
    vloc = nc.dram_tensor("vloc", [S, 128], BF16, kind="Internal").ap()
    oloc = nc.dram_tensor("oloc", [D, T], BF16, kind="Internal").ap()
    aag_in = nc.dram_tensor("aag_in", [T, D], BF16, kind="Internal").ap()
    aag_out = nc.dram_tensor("aag_out", [NCORE * T, D], BF16, kind="Internal").ap()

    with ExitStack() as stack:
        def mk_sbuf(stk, name, shape, dtype):
            return stk.enter_context(nc.sbuf_tensor(name, list(shape), dtype))
        def psum(name, shape, dtype):
            return stack.enter_context(nc.psum_tensor(name, list(shape), dtype))

        p = Prog(nc, stack)
        PS = [psum(f"ps{i}", [128, 512], F32) for i in range(6)]
        PB = [psum(f"pb{i}", [128, 1024], BF16) for i in range(2)]

        def sbuf(name, shape, dtype):
            return mk_sbuf(stack, name, shape, dtype)

        identF = sbuf("identF_sb", [128, 128], F32)
        identB = sbuf("identB_sb", [128, 128], BF16)
        trilB = sbuf("tril_sb", [128, 128], BF16)
        ones1F = sbuf("ones1F", [1, 128], F32)
        onescolB = sbuf("onescolB", [128, 1], BF16)
        ones2F = sbuf("ones2F", [128, 128], BF16)
        offtab = sbuf("offtab", [1, 64], I32)
        p.dma(identF[:, :], I["identF"], writes=["identF"])
        p.dma(identB[:, :], I["identB"], writes=["identB"])
        p.dma(trilB[:, :], I["tril"], writes=["tril"])
        p.dma(offtab[:, :], I["coff"], writes=["offtab"])
        p.op("DVE", lambda v: v.memset(ones1F[:, :], 1.0), writes=["ones1F"])
        p.op("DVE", lambda v: v.memset(onescolB[:, :], 1.0), writes=["onescolB"])
        p.op("DVE", lambda v: v.memset(ones2F[:, :], 2.0 / 128.0), writes=["ones2F"])

        yT = sbuf("yT", [128, ND * T], BF16)
        def yTt(k):
            return yT[:, k * T:(k + 1) * T]

        bq_sb = sbuf("bq_sb", [128, 16], F32)
        bk_sb = sbuf("bk_sb", [128, 8], F32)
        bv_sb = sbuf("bv_sb", [128, 8], F32)
        bo_sb = sbuf("bo_sb", [128, 16], F32)
        for nm, tt, t_ in (("abq", bq_sb, "bq_sb"), ("abk", bk_sb, "bk_sb"),
                           ("abv", bv_sb, "bv_sb"), ("abo", bo_sb, "bo_sb")):
            p.dma(tt[:, :], I[nm].rearrange("(o p) -> p o", p=128),
                  writes=[t_], allow_slow_non_contiguous=True)
        alpha1 = sbuf("alpha1", [1, 1], F32)
        alpha_col = sbuf("alpha_col", [128, 1], F32)
        p.dma(alpha1[:, :], I["malpha"].rearrange("(a b) -> a b", a=1), writes=["alpha1"])
        p.op("PE", lambda t: t.matmul(PS[2][:, 0:1], ones1F[:, :], alpha1[:, :],
                                      start=True, stop=True),
             reads=["alpha1", "ones1F"], writes=["ps2"])
        p.op("DVE", lambda v: v.tensor_copy(alpha_col[:, :], PS[2][:, 0:1]),
             reads=["ps2"], writes=["alpha_col"])
        mWgT_sb = sbuf("mWgT_sb", [128, 16 * NE], BF16)
        p.dma(mWgT_sb[:, :], I["mWgT"].rearrange("(k p) e -> p k e", p=128),
              writes=["mWgT"], eng="POOL")
        oneh_sb = sbuf("oneh_sb", [32, NE * 128], BF16)
        p.dma(oneh_sb[:, :], I["onehot"].rearrange("e p f -> p e f"),
              writes=["oneh"], allow_slow_non_contiguous=True)

        def emit_boundary(tag, src_rows, cbuf, cb_region, dst_rows, dst_region,
                          st, do_ln=True, lts=None, cb_w=T, cb_lt0=0, cb_f32=False,
                          src_region=None):
            xrow = mk_sbuf(st, f"{tag}_xrow", [128, D], F32)
            crow = mk_sbuf(st, f"{tag}_crow", [128, D], F32)
            yrow = mk_sbuf(st, f"{tag}_yrow", [128, D], BF16)
            sq = yrow
            colA = mk_sbuf(st, f"{tag}_colA", [128, 8], F32)
            if lts is None:
                lts = range(NT)
            for lt in lts:
                p.dma(xrow[:, :], src_rows[lt * 128:(lt + 1) * 128, :],
                      reads=[f"{src_region or tag + '_srcrows'}{lt}"], writes=[f"{tag}_xrow"])
                if cbuf is not None:
                    for o in range(ND):
                        c0 = o * cb_w + (lt - cb_lt0) * 128
                        if cb_f32:
                            bank = PS[o % 2]
                            rpb = f"ps{o % 2}"
                            p.op("PE", lambda t, c0=c0, bank=bank, cb=cbuf: t.transpose(
                                bank[:, 0:128], cb[:, c0:c0 + 128], identF[:, :]),
                                reads=[f"{cb_region}{o}", "identF"], writes=[rpb])
                        else:
                            bank = PB[o % 2]
                            rpb = f"pb{o % 2}"
                            p.op("PE", lambda t, c0=c0, bank=bank, cb=cbuf: t.transpose(
                                bank[:, 0:128], cb[:, c0:c0 + 128], identB[:, :]),
                                reads=[f"{cb_region}{o}", "identB"], writes=[rpb])
                        p.op("DVE", lambda v, o=o, bank=bank, cr=crow: v.tensor_copy(
                            cr[:, o * 128:(o + 1) * 128], bank[:, 0:128]),
                            reads=[rpb], writes=[f"{tag}_crow"])
                    p.op("DVE", lambda v, xr=xrow, cr=crow: v.tensor_add(
                        xr[:, :], xr[:, :], cr[:, :]),
                        reads=[f"{tag}_xrow", f"{tag}_crow"], writes=[f"{tag}_xrow"])
                p.dma(dst_rows[lt * 128:(lt + 1) * 128, :], xrow[:, :],
                      reads=[f"{tag}_xrow"], writes=[f"{dst_region}{lt}"])
                if not do_ln:
                    continue
                p.op("DVE", lambda v, xr=xrow, ca=colA: v.reduce_sum(
                    ca[:, 5:6], xr[:, :], axis=AX.X),
                    reads=[f"{tag}_xrow"], writes=[f"{tag}_cA"])
                p.op("DVE", lambda v, xr=xrow, sq=sq, ca=colA: v.scalar_tensor_tensor(
                    sq[:, :], xr[:, :], 1.0, xr[:, :], ALU.mult, ALU.mult,
                    accum_out=ca[:, 0:1]), reads=[f"{tag}_xrow"],
                    writes=[f"{tag}_yrow", f"{tag}_cA"])
                p.op("DVE", lambda v, ca=colA: v.tensor_scalar_mul(
                    ca[:, 5:6], ca[:, 5:6], 1.0 / D), reads=[f"{tag}_cA"], writes=[f"{tag}_cA"])
                p.op("DVE", lambda v, ca=colA: v.scalar_tensor_tensor(
                    ca[:, 1:2], ca[:, 5:6], -1.0, ca[:, 5:6], ALU.mult, ALU.mult),
                    reads=[f"{tag}_cA"], writes=[f"{tag}_cA"])
                p.op("DVE", lambda v, ca=colA: v.tensor_scalar(
                    ca[:, 2:3], ca[:, 0:1], 1.0 / D, EPS_LN, ALU.mult, ALU.add),
                    reads=[f"{tag}_cA"], writes=[f"{tag}_cA"])
                p.op("DVE", lambda v, ca=colA: v.tensor_tensor(
                    ca[:, 3:4], ca[:, 2:3], ca[:, 1:2], ALU.add),
                    reads=[f"{tag}_cA"], writes=[f"{tag}_cA"])
                p.op("ACT", lambda s, ca=colA: s.activation(
                    ca[:, 4:5], ca[:, 3:4], AF.Sqrt),
                    reads=[f"{tag}_cA"], writes=[f"{tag}_cA"])
                p.op("DVE", lambda v, ca=colA: v.reciprocal(ca[:, 6:7], ca[:, 4:5]),
                     reads=[f"{tag}_cA"], writes=[f"{tag}_cA"])
                p.op("DVE", lambda v, xr=xrow, yr=yrow, ca=colA: v.tensor_scalar(
                    yr[:, :], xr[:, :], ca[:, 5:6], ca[:, 6:7], ALU.subtract, ALU.mult),
                    reads=[f"{tag}_xrow", f"{tag}_cA"], writes=[f"{tag}_yrow"])
                for o in range(ND):
                    bank = PB[o % 2]
                    p.op("PE", lambda t, o=o, bank=bank, yr=yrow: t.transpose(
                        bank[:, 0:128], yr[:, o * 128:(o + 1) * 128], identB[:, :]),
                        reads=[f"{tag}_yrow", "identB"], writes=[f"pb{o % 2}"])
                    p.op("DVE", lambda v, o=o, lt=lt, bank=bank: v.tensor_copy(
                        yTt(o)[:, lt * 128:(lt + 1) * 128], bank[:, 0:128]),
                        reads=[f"pb{o % 2}"], writes=[f"yT{o}"])

        with ExitStack() as st0:
            emit_boundary("p0", I["x"], None, None, xr2, "xjunk", st0, do_ln=True)
            p.barrier()

        with ExitStack() as stA:
            kp_sb = mk_sbuf(stA, "kp_sb", [128, HKV * T], BF16)
            v_sb = mk_sbuf(stA, "v_sb", [128, HKV * T], BF16)

            with ExitStack() as stP:
                cosT = mk_sbuf(stP, "cosT", [128, T], BF16)
                sinT = mk_sbuf(stP, "sinT", [128, T], BF16)
                p.dma(cosT[:, :], I["ropec"], writes=["cosT"], eng="POOL")
                p.dma(sinT[:, :], I["ropes"], writes=["sinT"], eng="POOL")
                rT_sb = mk_sbuf(stP, "rT_sb", [128, 64], BF16)
                p.dma(rT_sb[:, :], I["aRT"], writes=["rT_sb"], eng="POOL")
                wbuf = [mk_sbuf(stP, f"wbuf{i}", [128, 16 * 128], BF16) for i in range(2)]
                kraw = mk_sbuf(stP, "kraw", [128, T], F32)
                krot = mk_sbuf(stP, "krot", [128, T], F32)
                ktmp = mk_sbuf(stP, "ktmp", [128, T], F32)
                fb_sb = mk_sbuf(stP, "fb_sb", [128, T], BF16)
                rown = mk_sbuf(stP, "rown", [1, T], F32)
                pc_sb = mk_sbuf(stP, "pc_sb", [64, T], F32)
                uc_sb = mk_sbuf(stP, "uc_sb", [64, 512], F32)
                ic_sb = mk_sbuf(stP, "ic_sb", [64, 512], I32)
                wc_sb = mk_sbuf(stP, "wc_sb", [64, 512], F32)
                gc_sb = mk_sbuf(stP, "gc_sb", [64, 512], F32)

                def proj_head(wi, out_emit):
                    wb = wbuf[wi]
                    rw = f"wbuf{wi}"
                    for tc in range(4):
                        bank = PS[4 + (tc % 2)]
                        rb = f"ps{4 + (tc % 2)}"
                        fns = []
                        for k in range(ND):
                            fns.append(lambda t, k=k, tc=tc, wb=wb, bank=bank: t.matmul(
                                bank[:, :], wb[:, k * 128:(k + 1) * 128],
                                yTt(k)[:, tc * 512:(tc + 1) * 512],
                                start=(k == 0), stop=(k == ND - 1)))
                        p.group("PE", fns, reads=[rw] + [f"yT{k}" for k in range(ND)],
                                writes=[rb])
                        out_emit(tc, bank, rb)

                def rope_feat():
                    p.op("DVE", lambda v, kr=krot, ka=kraw: v.tensor_scalar_mul(
                        kr[0:64, :], ka[64:128, :], -1.0), reads=["kraw"], writes=["krot"])
                    p.op("DVE", lambda v, kr=krot, ka=kraw: v.tensor_copy(
                        kr[64:128, :], ka[0:64, :]), reads=["kraw"], writes=["krot"])
                    p.op("DVE", lambda v, kt=ktmp, ka=kraw, ct=cosT: v.tensor_mul(
                        kt[:, :], ka[:, :], ct[:, :]), reads=["kraw", "cosT"], writes=["ktmp"])
                    p.op("DVE", lambda v, kr=krot, st_=sinT: v.tensor_mul(
                        kr[:, :], kr[:, :], st_[:, :]), reads=["krot", "sinT"], writes=["krot"])
                    p.op("DVE", lambda v, ka=kraw, kt=ktmp, kr=krot: v.tensor_add(
                        ka[:, :], kt[:, :], kr[:, :]), reads=["ktmp", "krot"], writes=["kraw"])
                    p.op("DVE", lambda v, fb=fb_sb, ka=kraw: v.scalar_tensor_tensor(
                        fb[:, :], ka[:, :], 1.0, ka[:, :], ALU.mult, ALU.mult),
                        reads=["kraw"], writes=["fb"])
                    for tc in range(4):
                        p.op("PE", lambda t, tc=tc, fb=fb_sb: t.matmul(
                            PS[4][0:1, :], onescolB[:, :], fb[:, tc * 512:(tc + 1) * 512],
                            start=True, stop=True), reads=["fb", "onescolB"], writes=["ps4"])
                        p.op("DVE", lambda v, tc=tc, rn=rown: v.tensor_scalar_max(
                            rn[:, tc * 512:(tc + 1) * 512], PS[4][0:1, :], 1e-12),
                            reads=["ps4"], writes=["rown"])
                    p.op("ACT", lambda s, rn=rown: s.activation(rn[:, :], rn[:, :], AF.Sqrt),
                         reads=["rown"], writes=["rown"])
                    p.op("DVE", lambda v, rn=rown: v.reciprocal(rn[:, :], rn[:, :]),
                         reads=["rown"], writes=["rown"])
                    for tc in range(4):
                        p.op("PE", lambda t, tc=tc, rn=rown: t.matmul(
                            PS[4][:, :], ones1F[:, :], rn[:, tc * 512:(tc + 1) * 512],
                            start=True, stop=True), reads=["rown", "ones1F"], writes=["ps4"])
                        p.op("DVE", lambda v, tc=tc, fb=fb_sb, ka=kraw: v.tensor_mul(
                            fb[:, tc * 512:(tc + 1) * 512],
                            ka[:, tc * 512:(tc + 1) * 512], PS[4][:, :]),
                            reads=["ps4", "kraw", "fb"], writes=["fb"])
                    for tc in range(4):
                        p.op("PE", lambda t, tc=tc, rt=rT_sb, fb=fb_sb: t.matmul(
                            PS[5][0:64, :], rt[:, :], fb[:, tc * 512:(tc + 1) * 512],
                            start=True, stop=True), reads=["fb", "rT_sb"], writes=["ps5"])
                        p.op("DVE", lambda v, tc=tc, pc=pc_sb: v.tensor_copy(
                            pc[:, tc * 512:(tc + 1) * 512], PS[5][0:64, :]),
                            reads=["ps5"], writes=["pc"])
                    for tc in range(4):
                        sl = slice(tc * 512, (tc + 1) * 512)
                        for phase, shift, bias in ((0, 0.5, 0.0), (1, 0.75, np.pi / 2)):
                            p.op("DVE", lambda v, sl=sl, uc=uc_sb, pc=pc_sb, sh=shift: v.tensor_scalar(
                                uc[:, :], pc[:, sl], INV2PI, sh, ALU.mult, ALU.add),
                                reads=["pc"], writes=["uc"])
                            p.op("DVE", lambda v, ic=ic_sb, uc=uc_sb: v.tensor_copy(
                                ic[:, :], uc[:, :]), reads=["uc"], writes=["ic"])
                            p.op("DVE", lambda v, wc=wc_sb, ic=ic_sb: v.tensor_copy(
                                wc[:, :], ic[:, :]), reads=["ic"], writes=["wc"])
                            p.op("DVE", lambda v, gc=gc_sb, wc=wc_sb, uc=uc_sb: v.tensor_tensor(
                                gc[:, :], wc[:, :], uc[:, :], ALU.is_gt),
                                reads=["wc", "uc"], writes=["gc"])
                            p.op("DVE", lambda v, wc=wc_sb, gc=gc_sb: v.tensor_sub(
                                wc[:, :], wc[:, :], gc[:, :]), reads=["wc", "gc"], writes=["wc"])
                            if bias == 0.0:
                                p.op("DVE", lambda v, sl=sl, wc=wc_sb, pc=pc_sb: v.scalar_tensor_tensor(
                                    wc[:, :], wc[:, :], -TWOPI, pc[:, sl], ALU.mult, ALU.add),
                                    reads=["wc", "pc"], writes=["wc"])
                            else:
                                p.op("DVE", lambda v, sl=sl, uc=uc_sb, pc=pc_sb, bi=bias: v.tensor_scalar_add(
                                    uc[:, :], pc[:, sl], bi), reads=["pc"], writes=["uc"])
                                p.op("DVE", lambda v, wc=wc_sb, uc=uc_sb: v.scalar_tensor_tensor(
                                    wc[:, :], wc[:, :], -TWOPI, uc[:, :], ALU.mult, ALU.add),
                                    reads=["wc", "uc"], writes=["wc"])
                            p.op("DVE", lambda v, wc=wc_sb: v.tensor_scalar(
                                wc[:, :], wc[:, :], -np.pi, np.pi, ALU.max, ALU.min),
                                reads=["wc"], writes=["wc"])
                            r0 = 0 if phase == 0 else 64
                            p.op("ACT", lambda s, sl=sl, fb=fb_sb, wc=wc_sb, r0=r0: s.activation(
                                fb[r0:r0 + 64, sl], wc[:, :], AF.Sin),
                                reads=["wc", "fb"], writes=["fb"])

                for g in range(HKV):
                    wi = g % 2
                    p.dma(wbuf[wi][:, :],
                          I["aWkT"].rearrange("(k p) o -> p k o", p=128)[:, :, g * 128:(g + 1) * 128],
                          writes=[f"wbuf{wi}"], eng="POOL")

                    def k_evac(tc, bank, rb, g=g):
                        p.op("ACT", lambda s, tc=tc, bank=bank, g=g, ka=kraw: s.activation(
                            ka[:, tc * 512:(tc + 1) * 512], bank[:, :], AF.Identity,
                            bias=bk_sb[:, g:g + 1]), reads=[rb, "bk_sb"], writes=["kraw"])
                    proj_head(wi, k_evac)
                    rope_feat()
                    p.op("DVE", lambda v, g=g, kp=kp_sb, fb=fb_sb: v.tensor_copy(
                        kp[:, g * T:(g + 1) * T], fb[:, :]), reads=["fb"], writes=[f"kp{g}"])
                    p.dma(wbuf[wi][:, :],
                          I["aWvT"].rearrange("(k p) o -> p k o", p=128)[:, :, g * 128:(g + 1) * 128],
                          writes=[f"wbuf{wi}"], eng="POOL")

                    def v_evac(tc, bank, rb, g=g):
                        p.op("ACT", lambda s, tc=tc, bank=bank, g=g, vs=v_sb: s.activation(
                            vs[:, g * T + tc * 512:g * T + (tc + 1) * 512], bank[:, :],
                            AF.Identity, bias=bv_sb[:, g:g + 1]),
                            reads=[rb, "bv_sb"], writes=[f"v{g}"])
                    proj_head(wi, v_evac)

                for h in range(HA):
                    wi = h % 2
                    p.dma(wbuf[wi][:, :],
                          I["aWqT"].rearrange("(k p) o -> p k o", p=128)[:, :, h * 128:(h + 1) * 128],
                          writes=[f"wbuf{wi}"], eng="POOL")

                    def q_evac(tc, bank, rb, h=h):
                        p.op("ACT", lambda s, tc=tc, bank=bank, h=h, ka=kraw: s.activation(
                            ka[:, tc * 512:(tc + 1) * 512], bank[:, :], AF.Identity,
                            bias=bq_sb[:, h:h + 1]), reads=[rb, "bq_sb"], writes=["kraw"])
                    proj_head(wi, q_evac)
                    rope_feat()
                    p.dma(q_shard[h * 128:(h + 1) * 128, :], fb_sb[:, :],
                          reads=["fb"], writes=[f"qp{h}"])
                p.barrier()

            with ExitStack() as stM:
                qp_t = mk_sbuf(stM, "qp_t", [128, T], BF16)
                prod = mk_sbuf(stM, "prod", [128, T], BF16)
                term = [mk_sbuf(stM, f"term{i}", [128, 512], BF16) for i in range(2)]
                o_col = mk_sbuf(stM, "o_col", [128, T], BF16)
                o_row = mk_sbuf(stM, "o_row", [128, T], BF16)
                for h in range(HA):
                    p.dma(qp_t[:, :], q_shard[h * 128:(h + 1) * 128, :],
                          reads=[f"qp{h}"], writes=["qp_t"])
                    for g in range(HKV):
                        p.op("DVE", lambda v, g=g, pr=prod, qt=qp_t, kp=kp_sb: v.tensor_mul(
                            pr[:, :], qt[:, :], kp[:, g * T:(g + 1) * T]),
                            reads=["qp_t", f"kp{g}"], writes=["prod"])
                        for tc in range(4):
                            ba = PS[4 + (tc % 2)]
                            rba = f"ps{4 + (tc % 2)}"
                            p.op("PE", lambda t, tc=tc, ba=ba, pr=prod: t.matmul(
                                ba[:, :], ones2F[:, :], pr[:, tc * 512:(tc + 1) * 512],
                                start=True, stop=True), reads=["prod", "ones2F"], writes=[rba])
                            ti = tc % 2
                            p.op("DVE", lambda v, tc=tc, g=g, ba=ba, tm=term[ti], vs=v_sb: v.tensor_mul(
                                tm[:, :], ba[:, :],
                                vs[:, g * T + tc * 512:g * T + (tc + 1) * 512]),
                                reads=[rba, f"v{g}"], writes=[f"term{ti}"])
                            p.op("PE", lambda t, tc=tc, g=g, tm=term[ti]: t.matmul(
                                PS[tc][:, :], identB[:, :], tm[:, :],
                                start=(g == 0), stop=(g == HKV - 1)),
                                reads=[f"term{ti}", "identB"], writes=[f"ps{tc}"])
                    for tc in range(4):
                        p.op("ACT", lambda s, tc=tc, oc=o_col: s.copy(
                            oc[:, tc * 512:(tc + 1) * 512], PS[tc][:, :]),
                            reads=[f"ps{tc}"], writes=["o_col"])
                    for lt in range(NT):
                        bank = PB[lt % 2]
                        p.op("PE", lambda t, lt=lt, bank=bank, oc=o_col: t.transpose(
                            bank[:, 0:128], oc[:, lt * 128:(lt + 1) * 128], identB[:, :]),
                            reads=["o_col", "identB"], writes=[f"pb{lt % 2}"])
                        p.op("DVE", lambda v, lt=lt, bank=bank, orw=o_row: v.tensor_copy(
                            orw[:, lt * 128:(lt + 1) * 128], bank[:, 0:128]),
                            reads=[f"pb{lt % 2}"], writes=["o_row"])
                    dst = bass.AP(aag_in.tensor, h * 128 * D,
                                  [[D, 8], [128, 16], [8 * D, 16], [1, 128]])
                    p.dma(dst, o_row[:, :], reads=["o_row"], writes=[f"oag{h}"])
                p.barrier()

        p.collective("AllGather", ALU.bypass, [aag_in], [aag_out],
                     reads=[f"oag{h}" for h in range(HA)], writes=["aag_out"])

        with ExitStack() as stG:
            scr = mk_sbuf(stG, "scr", [128, D], BF16)
            for ltp in range(NT):
                b, k = ltp // 8, ltp % 8
                const = (2 * k * 128 + 8 * b * 8) * D
                def src_ap(e, const=const):
                    off = p.emit_off(e, "SP", const, cq=128 * D, cm=16 * D)
                    return bass.AP(aag_out.tensor, off,
                                   [[T * D, 8], [D, 16], [1, D]])
                p.dma(scr[:, :], src_ap, reads=["aag_out"], writes=["scr"])
                for o in range(ND):
                    bank = PB[o % 2]
                    p.op("PE", lambda t, o=o, bank=bank, sc=scr: t.transpose(
                        bank[:, 0:128], sc[:, o * 128:(o + 1) * 128], identB[:, :]),
                        reads=["scr", "identB"], writes=[f"pb{o % 2}"])
                    p.op("DVE", lambda v, o=o, ltp=ltp, bank=bank: v.tensor_copy(
                        bass.AP(yT, o * T + ltp * 128,
                                [[ND * T, 128], [8, 8], [64, 2], [1, 8]]),
                        bass.AP(bank, 0, [[1024, 128], [16, 8], [8, 2], [1, 8]])),
                        reads=[f"pb{o % 2}"], writes=[f"yT{o}"])
            p.barrier()

        stWB = stack.enter_context(ExitStack())
        cbuf = mk_sbuf(stWB, "cbuf", [128, ND * T], BF16)
        with ExitStack() as stW:
            wbuf2 = [mk_sbuf(stW, f"wobuf{i}", [128, 16 * 128], BF16) for i in range(2)]
            for o in range(ND):
                wi = o % 2
                p.dma(wbuf2[wi][:, :],
                      I["aWoT"].rearrange("(k p) o -> p k o", p=128)[:, :, o * 128:(o + 1) * 128],
                      writes=[f"wobuf{wi}"], eng="POOL")
                for tc in range(4):
                    bank = PS[4 + (tc % 2)]
                    rb = f"ps{4 + (tc % 2)}"
                    fns = []
                    for k in range(ND):
                        fns.append(lambda t, k=k, tc=tc, wb=wbuf2[wi], bank=bank: t.matmul(
                            bank[:, :], wb[:, k * 128:(k + 1) * 128],
                            yTt(k)[:, tc * 512:(tc + 1) * 512],
                            start=(k == 0), stop=(k == ND - 1)))
                    p.group("PE", fns, reads=[f"wobuf{wi}"] + [f"yT{k}" for k in range(ND)],
                            writes=[rb])
                    p.op("ACT", lambda s, tc=tc, o=o, bank=bank, cb=cbuf: s.activation(
                        cb[:, o * T + tc * 512:o * T + (tc + 1) * 512], bank[:, :],
                        AF.Identity, bias=bo_sb[:, o:o + 1]),
                        reads=[rb, "bo_sb"], writes=[f"cb{o}"])
            p.barrier()

        with ExitStack() as st1:
            emit_boundary("b1", I["x"], cbuf, "cb", xr1, "xr1", st1, do_ln=True)
            p.barrier()
        stWB.close()

        def emit_ffn(tag, src_rows, src_region, dst_rows, dst_region, st,
                     do_ln=True, final_out=False):
            combT = mk_sbuf(st, f"{tag}_combT", [32, T], BF16)
            with ExitStack() as sr:
                lg = mk_sbuf(sr, f"{tag}_lg", [128, NE], F32)
                pr = mk_sbuf(sr, f"{tag}_pr", [128, NE], F32)
                tw = mk_sbuf(sr, f"{tag}_tw", [128, NE], F32)
                pm = mk_sbuf(sr, f"{tag}_pm", [128, NE], F32)
                act = mk_sbuf(sr, f"{tag}_act", [128, NE], F32)
                mc = mk_sbuf(sr, f"{tag}_mc", [128, 16], F32)
                for lt in range(NT):
                    fns = []
                    for k in range(ND):
                        fns.append(lambda t, k=k, lt=lt: t.matmul(
                            PS[3][:, 0:NE], yTt(k)[:, lt * 128:(lt + 1) * 128],
                            mWgT_sb[:, k * NE:(k + 1) * NE],
                            start=(k == 0), stop=(k == ND - 1)))
                    p.group("PE", fns, reads=["mWgT"] + [f"yT{k}" for k in range(ND)],
                            writes=["ps3"])
                    R = f"{tag}_r"
                    p.op("DVE", lambda v, lg=lg: v.tensor_copy(lg[:, :], PS[3][:, 0:NE]),
                         reads=["ps3"], writes=[R])
                    p.op("DVE", lambda v, lg=lg, mc=mc: v.reduce_max(
                        mc[:, 0:1], lg[:, :], axis=AX.X), reads=[R], writes=[R])
                    p.op("DVE", lambda v, mc=mc: v.tensor_scalar_mul(
                        mc[:, 1:2], mc[:, 0:1], -1.0), reads=[R], writes=[R])
                    p.op("ACT", lambda s, lg=lg, pr=pr, mc=mc: s.activation(
                        pr[:, :], lg[:, :], AF.Exp, bias=mc[:, 1:2],
                        accum_out=mc[:, 2:3]), reads=[R], writes=[R])
                    p.op("DVE", lambda v, mc=mc: v.reciprocal(mc[:, 3:4], mc[:, 2:3]),
                         reads=[R], writes=[R])
                    p.op("DVE", lambda v, pr=pr, mc=mc: v.tensor_scalar_mul(
                        pr[:, :], pr[:, :], mc[:, 3:4]), reads=[R], writes=[R])
                    p.op("DVE", lambda v, tw=tw, pr=pr: v.tensor_copy(tw[:, :], pr[:, :]),
                         reads=[R], writes=[R])
                    for i8 in range(TOPK):
                        p.op("DVE", lambda v, tw=tw, mc=mc, i8=i8: v.reduce_max(
                            mc[:, 8 + i8:9 + i8], tw[:, :], axis=AX.X),
                            reads=[R], writes=[R])
                        if i8 < TOPK - 1:
                            p.op("DVE", lambda v, pm=pm, tw=tw, mc=mc, i8=i8: v.tensor_scalar(
                                pm[:, :], tw[:, :], mc[:, 8 + i8:9 + i8], None, ALU.is_ge),
                                reads=[R], writes=[R])
                            p.op("DVE", lambda v, tw=tw, pm=pm: v.scalar_tensor_tensor(
                                tw[:, :], pm[:, :], -2.0, tw[:, :], ALU.mult, ALU.add),
                                reads=[R], writes=[R])
                    p.op("DVE", lambda v, pm=pm, pr=pr, mc=mc: v.tensor_scalar(
                        pm[:, :], pr[:, :], mc[:, 15:16], None, ALU.is_ge),
                        reads=[R], writes=[R])
                    p.op("DVE", lambda v, act=act, pr=pr, pm=pm, mc=mc: v.scalar_tensor_tensor(
                        act[:, :], pr[:, :], 1.0, pm[:, :], ALU.mult, ALU.mult,
                        accum_out=mc[:, 4:5]), reads=[R], writes=[R])
                    p.op("DVE", lambda v, mc=mc: v.tensor_scalar_mul(
                        mc[:, 5:6], mc[:, 4:5], TAU), reads=[R, "alpha_col"], writes=[R])
                    p.op("DVE", lambda v, act=act, pr=pr, mc=mc: v.tensor_scalar(
                        act[:, :], pr[:, :], alpha_col[:, 0:1], mc[:, 5:6],
                        ALU.mult, ALU.subtract), reads=[R, "alpha_col"], writes=[R])
                    p.op("DVE", lambda v, act=act: v.tensor_single_scalar(
                        act[:, :], act[:, :], 0.0, ALU.is_ge), reads=[R], writes=[R])
                    p.op("DVE", lambda v, act=act, pm=pm: v.tensor_mul(
                        act[:, :], act[:, :], pm[:, :]), reads=[R], writes=[R])
                    p.op("DVE", lambda v, act=act, mc=mc: v.reduce_sum(
                        mc[:, 6:7], act[:, :], axis=AX.X), reads=[R], writes=[R])
                    p.op("DVE", lambda v, mc=mc: v.tensor_single_scalar(
                        mc[:, 6:7], mc[:, 6:7], 0.5, ALU.is_lt), reads=[R], writes=[R])
                    p.op("DVE", lambda v, mc=mc: v.tensor_single_scalar(
                        mc[:, 7:8], alpha_col[:, 0:1], 0.0, ALU.is_ge),
                        reads=["alpha_col"], writes=[R])
                    p.op("DVE", lambda v, mc=mc: v.tensor_mul(
                        mc[:, 1:2], mc[:, 7:8], mc[:, 8:9]), reads=[R], writes=[R])
                    p.op("DVE", lambda v, mc=mc: v.tensor_scalar(
                        mc[:, 2:3], mc[:, 7:8], -1.0, 1.0, ALU.mult, ALU.add),
                        reads=[R], writes=[R])
                    p.op("DVE", lambda v, mc=mc: v.tensor_mul(
                        mc[:, 2:3], mc[:, 2:3], mc[:, 15:16]), reads=[R], writes=[R])
                    p.op("DVE", lambda v, mc=mc: v.tensor_add(
                        mc[:, 1:2], mc[:, 1:2], mc[:, 2:3]), reads=[R], writes=[R])
                    p.op("DVE", lambda v, tw=tw, pr=pr, mc=mc: v.tensor_scalar(
                        tw[:, :], pr[:, :], mc[:, 1:2], None, ALU.is_equal),
                        reads=[R], writes=[R])
                    p.op("DVE", lambda v, tw=tw, pm=pm: v.tensor_mul(
                        tw[:, :], tw[:, :], pm[:, :]), reads=[R], writes=[R])
                    p.op("DVE", lambda v, tw=tw, mc=mc: v.tensor_scalar_mul(
                        tw[:, :], tw[:, :], mc[:, 6:7]), reads=[R], writes=[R])
                    p.op("DVE", lambda v, act=act, tw=tw: v.tensor_add(
                        act[:, :], act[:, :], tw[:, :]), reads=[R], writes=[R])
                    p.op("DVE", lambda v, pm=pm, pr=pr, act=act, mc=mc: v.scalar_tensor_tensor(
                        pm[:, :], pr[:, :], 1.0, act[:, :], ALU.mult, ALU.mult,
                        accum_out=mc[:, 4:5]), reads=[R], writes=[R])
                    p.op("DVE", lambda v, mc=mc: v.tensor_single_scalar(
                        mc[:, 4:5], mc[:, 4:5], 1e-12, ALU.max), reads=[R], writes=[R])
                    p.op("DVE", lambda v, mc=mc: v.reciprocal(
                        mc[:, 5:6], mc[:, 4:5]), reads=[R], writes=[R])
                    p.op("DVE", lambda v, pm=pm, mc=mc: v.tensor_scalar_mul(
                        pm[:, :], pm[:, :], mc[:, 5:6]), reads=[R], writes=[R])
                    p.op("PE", lambda t, pm=pm: t.transpose(
                        PS[2][0:NE, 0:128], pm[:, :], identF[:, :]),
                        reads=[R, "identF"], writes=["ps2"])
                    p.op("DVE", lambda v, lt=lt, cT=combT: v.tensor_copy(
                        cT[:, lt * 128:(lt + 1) * 128], PS[2][0:NE, 0:128]),
                        reads=["ps2"], writes=[f"{tag}_combT"])

            acc = mk_sbuf(st, f"{tag}_acc", [128, ND * 1024], F32)
            for hf in range(2):
              with ExitStack() as sx:
                gsil = mk_sbuf(sx, f"{tag}_gsil{hf}", [128, 1024], F32)
                ht = mk_sbuf(sx, f"{tag}_ht{hf}", [128, 512], F32)
                hp = [mk_sbuf(sx, f"{tag}_hp{hf}{i}", [128, 6 * 1024], BF16) for i in range(2)]
                bcc = [mk_sbuf(sx, f"{tag}_bcc{hf}{i}", [128, 1024], BF16) for i in range(2)]
                wgu0 = mk_sbuf(sx, f"{tag}_wgu{hf}", [128, 16 * 128], BF16)
                wgu = [wgu0, wgu0]
                wdb = [mk_sbuf(sx, f"{tag}_wdb{hf}{i}", [128, 6 * 512], BF16) for i in range(2)]
                t0 = hf * 1024
                p.op("DVE", lambda v, acc=acc: v.memset(acc[:, :], 0.0),
                     writes=[f"{tag}_acc"])
                for grp in range(NE // 2):
                    for ei in range(2):
                        e = grp * 2 + ei
                        for cc2 in range(2):
                            p.op("PE", lambda t, e=e, cc2=cc2, t0=t0, cT=combT: t.matmul(
                                PS[2][:, :], oneh_sb[:, e * 128:(e + 1) * 128],
                                cT[:, t0 + cc2 * 512:t0 + (cc2 + 1) * 512],
                                start=True, stop=True),
                                reads=["oneh", f"{tag}_combT"], writes=["ps2"])
                            p.op("DVE", lambda v, ei=ei, cc2=cc2, bc=bcc[ei]: v.tensor_copy(
                                bc[:, cc2 * 512:(cc2 + 1) * 512], PS[2][:, :]),
                                reads=["ps2"], writes=[f"{tag}_bcc{ei}"])
                        for jj in range(6):
                            p.dma(wgu[ei][:, :],
                                  I["mWguT"][e].rearrange("(k p) j -> p k j", p=128)[:, :, jj * 128:(jj + 1) * 128],
                                  writes=[f"{tag}_wgu"], eng="POOL")
                            for tc in range(2):
                                bank = PS[4 + (tc % 2)]
                                rb = f"ps{4 + (tc % 2)}"
                                fns = []
                                for k in range(ND):
                                    fns.append(lambda t, k=k, tc=tc, t0=t0, wb=wgu[ei], bank=bank: t.matmul(
                                        bank[:, :], wb[:, k * 128:(k + 1) * 128],
                                        yTt(k)[:, t0 + tc * 512:t0 + (tc + 1) * 512],
                                        start=(k == 0), stop=(k == ND - 1)))
                                p.group("PE", fns, reads=[f"{tag}_wgu"] + [f"yT{k}" for k in range(ND)],
                                        writes=[rb])
                                p.op("ACT", lambda s, tc=tc, bank=bank, gs=gsil: s.activation(
                                    gs[:, tc * 512:(tc + 1) * 512], bank[:, :], AF.Silu),
                                    reads=[rb], writes=[f"{tag}_gsil"])
                            p.dma(wgu[ei][:, :],
                                  I["mWguT"][e].rearrange("(k p) j -> p k j", p=128)[:, :, 768 + jj * 128:768 + (jj + 1) * 128],
                                  writes=[f"{tag}_wgu"], eng="POOL")
                            for tc in range(2):
                                bank = PS[4 + (tc % 2)]
                                rb = f"ps{4 + (tc % 2)}"
                                fns = []
                                for k in range(ND):
                                    fns.append(lambda t, k=k, tc=tc, t0=t0, wb=wgu[ei], bank=bank: t.matmul(
                                        bank[:, :], wb[:, k * 128:(k + 1) * 128],
                                        yTt(k)[:, t0 + tc * 512:t0 + (tc + 1) * 512],
                                        start=(k == 0), stop=(k == ND - 1)))
                                p.group("PE", fns, reads=[f"{tag}_wgu"] + [f"yT{k}" for k in range(ND)],
                                        writes=[rb])
                                p.op("DVE", lambda v, tc=tc, bank=bank, gs=gsil, h1=ht: v.tensor_mul(
                                    h1[:, :], gs[:, tc * 512:(tc + 1) * 512], bank[:, :]),
                                    reads=[rb, f"{tag}_gsil"], writes=[f"{tag}_ht"])
                                p.op("DVE", lambda v, tc=tc, jj=jj, h1=ht, hpp=hp[ei], bc=bcc[ei]: v.tensor_mul(
                                    hpp[:, jj * 1024 + tc * 512:jj * 1024 + (tc + 1) * 512],
                                    h1[:, :], bc[:, tc * 512:(tc + 1) * 512]),
                                    reads=[f"{tag}_ht", f"{tag}_bcc{ei}"], writes=[f"{tag}_hp{ei}"])
                    for dmc in range(4):
                        for ei in range(2):
                            e = grp * 2 + ei
                            p.dma(wdb[ei][:, :],
                                  I["mWdT"][e].rearrange("(k p) m -> p k m", p=128)[:, :, dmc * 512:(dmc + 1) * 512],
                                  writes=[f"{tag}_wdb{ei}"], eng="POOL")
                        for dms in range(4):
                            dm = dmc * 4 + dms
                            for tc in range(2):
                                fns = []
                                for ei in range(2):
                                    for k in range(6):
                                        fns.append(lambda t, ei=ei, k=k, tc=tc, dms=dms, wd=wdb[ei], hpp=hp[ei]: t.matmul(
                                            PS[tc][:, :], wd[:, k * 512 + dms * 128:k * 512 + (dms + 1) * 128],
                                            hpp[:, k * 1024 + tc * 512:k * 1024 + (tc + 1) * 512],
                                            start=(ei == 0 and k == 0), stop=(ei == 1 and k == 5)))
                                p.group("PE", fns,
                                        reads=[f"{tag}_wdb0", f"{tag}_wdb1", f"{tag}_hp0", f"{tag}_hp1"],
                                        writes=[f"ps{tc}"])
                                p.op("DVE", lambda v, dm=dm, tc=tc, acc=acc: v.tensor_tensor(
                                    acc[:, dm * 1024 + tc * 512:dm * 1024 + (tc + 1) * 512],
                                    acc[:, dm * 1024 + tc * 512:dm * 1024 + (tc + 1) * 512],
                                    PS[tc][:, :], ALU.add),
                                    reads=[f"ps{tc}", f"{tag}_acc"], writes=[f"{tag}_acc"])
                p.barrier()
              for o in range(ND):
                  p.writer[f"{tag}_accv{o}"] = p.writer.get(f"{tag}_acc")
              with ExitStack() as sb2:
                  emit_boundary(f"{tag}h{hf}", src_rows, acc, f"{tag}_accv",
                                dst_rows, dst_region, sb2,
                                do_ln=do_ln, lts=range(hf * 8, (hf + 1) * 8),
                                cb_w=1024, cb_lt0=hf * 8, cb_f32=True,
                                src_region=src_region)
              p.barrier()

        with ExitStack() as st2:
            emit_ffn("f1", xr1, "xr1", xr2, "xr2", st2, do_ln=True)


        with ExitStack() as s3:
            colkv = mk_sbuf(s3, "colkv", [128, 4], F32)
            colq = mk_sbuf(s3, "colq", [128, 12], F32)
            p.dma(colkv[:, :], I["bkvnw"].rearrange("(o p) -> p o", p=128),
                  writes=["colkv"], allow_slow_non_contiguous=True)
            p.dma(colq[:, :], I["bqnw"].rearrange("(o p) -> p o", p=128),
                  writes=["colq"], allow_slow_non_contiguous=True)
            fcs = mk_sbuf(s3, "fcs", [16, 32], F32)
            fss = mk_sbuf(s3, "fss", [16, 32], F32)
            p.dma(fcs[:, :], I["bfc"], writes=["fcs"])
            p.dma(fss[:, :], I["bfs"], writes=["fss"])
            fcT = mk_sbuf(s3, "fcT", [32, 16], F32)
            fsT = mk_sbuf(s3, "fsT", [32, 16], F32)
            nfsT = mk_sbuf(s3, "nfsT", [32, 16], F32)
            p.op("PE", lambda t: t.transpose(PS[2][0:32, 0:16], fcs[:, :], identF[0:16, 0:16]),
                 reads=["fcs", "identF"], writes=["ps2"])
            p.op("DVE", lambda v: v.tensor_copy(fcT[:, :], PS[2][0:32, 0:16]),
                 reads=["ps2"], writes=["fcT"])
            p.op("PE", lambda t: t.transpose(PS[2][0:32, 0:16], fss[:, :], identF[0:16, 0:16]),
                 reads=["fss", "identF"], writes=["ps2"])
            p.op("DVE", lambda v: v.tensor_copy(fsT[:, :], PS[2][0:32, 0:16]),
                 reads=["ps2"], writes=["fsT"])
            p.op("DVE", lambda v: v.tensor_scalar_mul(nfsT[:, :], fsT[:, :], -1.0),
                 reads=["fsT"], writes=["nfsT"])

            wch = [mk_sbuf(s3, f"b_wch{i}", [128, 16 * 128], BF16) for i in range(2)]
            stg = mk_sbuf(s3, "b_stg", [128, T], BF16)
            sqb = mk_sbuf(s3, "b_sqb", [128, T], BF16)
            rowA = mk_sbuf(s3, "b_rowA", [1, T], F32)
            rstd_bc = mk_sbuf(s3, "b_rstdbc", [128, T], F32)
            nkvn = mk_sbuf(s3, "b_nkvn", [128, 4 * T], BF16)

            def rows_finalize(nfeat):
                p.op("DVE", lambda v, rw=rowA, nf=nfeat: v.tensor_scalar(
                    rw[:, :], rw[:, :], 1.0 / nf, EPS_LN, ALU.mult, ALU.add),
                    reads=["rowA"], writes=["rowA"])
                p.op("ACT", lambda s, rw=rowA: s.activation(rw[:, :], rw[:, :], AF.Sqrt),
                     reads=["rowA"], writes=["rowA"])
                p.op("DVE", lambda v, rw=rowA: v.reciprocal(rw[:, :], rw[:, :]),
                     reads=["rowA"], writes=["rowA"])
                for tc in range(4):
                    p.op("PE", lambda t, tc=tc, rw=rowA: t.matmul(
                        PS[2][:, :], ones1F[:, :], rw[:, tc * 512:(tc + 1) * 512],
                        start=True, stop=True), reads=["rowA", "ones1F"], writes=["ps2"])
                    p.op("DVE", lambda v, tc=tc, rb=rstd_bc: v.tensor_copy(
                        rb[:, tc * 512:(tc + 1) * 512], PS[2][:, :]),
                        reads=["ps2"], writes=["rstd_bc"])

            def bigproj(nsrc, src_sb, src_region, nK, wname, wI, colw, ncols_w,
                        out_emit, oo, scale_col=True):
                wb = wch[oo % 2]
                rw = f"b_wch{oo % 2}"
                p.dma(wb[:, 0:nK * 128],
                      wI.rearrange("(k p) o -> p k o", p=128)[:, :, oo * 128:(oo + 1) * 128],
                      writes=[rw], eng="POOL")
                if scale_col:
                    for k in range(nK):
                        p.op("DVE", lambda v, k=k, wb=wb, cw=colw: v.tensor_scalar_mul(
                            wb[:, k * 128:(k + 1) * 128], wb[:, k * 128:(k + 1) * 128],
                            cw[:, k:k + 1]), reads=[rw, ncols_w], writes=[rw])
                for tc in range(4):
                    bank = PS[4 + (tc % 2)]
                    rb = f"ps{4 + (tc % 2)}"
                    fns = []
                    for k in range(nK):
                        fns.append(lambda t, k=k, tc=tc, wb=wb, bank=bank, ss=src_sb: t.matmul(
                            bank[:, :], wb[:, k * 128:(k + 1) * 128],
                            (yTt(k) if ss is None else ss[:, k * T:(k + 1) * T])[:, tc * 512:(tc + 1) * 512],
                            start=(k == 0), stop=(k == nK - 1)))
                    p.group("PE", fns,
                            reads=[rw] + ([f"yT{k}" for k in range(ND)] if src_sb is None else [src_region]),
                            writes=[rb])
                    out_emit(tc, bank, rb)

            with ExitStack() as s3a:
                nkv_f = mk_sbuf(s3a, "b_nkvf", [128, 4 * T], F32)
                for o in range(4):
                    def nkv_evac(tc, bank, rb, o=o):
                        p.op("DVE", lambda v, o=o, tc=tc, bank=bank, nf=nkv_f: v.tensor_copy(
                            nf[:, o * T + tc * 512:o * T + (tc + 1) * 512], bank[:, :]),
                            reads=[rb], writes=["b_nkvf"])
                    bigproj(None, None, None, ND, None, I["bWckv"], None, None,
                            nkv_evac, o, scale_col=False)
                    p.op("ACT", lambda s, o=o, nf=nkv_f, sq=sqb: s.activation(
                        sq[:, :], nf[:, o * T:(o + 1) * T], AF.Square),
                        reads=["b_nkvf"], writes=["b_sqb"])
                    for tc in range(4):
                        p.op("PE", lambda t, tc=tc, sq=sqb: t.matmul(
                            PS[3][0:1, :], onescolB[:, :], sq[:, tc * 512:(tc + 1) * 512],
                            start=True, stop=True),
                            reads=["b_sqb", "onescolB"], writes=["ps3"])
                        if o == 0:
                            p.op("DVE", lambda v, tc=tc, rw=rowA: v.tensor_copy(
                                rw[:, tc * 512:(tc + 1) * 512], PS[3][0:1, :]),
                                reads=["ps3"], writes=["rowA"])
                        else:
                            p.op("DVE", lambda v, tc=tc, rw=rowA: v.tensor_tensor(
                                rw[:, tc * 512:(tc + 1) * 512], rw[:, tc * 512:(tc + 1) * 512],
                                PS[3][0:1, :], ALU.add), reads=["ps3", "rowA"], writes=["rowA"])
                rows_finalize(R_KV)
                for o in range(4):
                    p.op("DVE", lambda v, o=o, nf=nkv_f, nk=nkvn, rb=rstd_bc: v.tensor_mul(
                        nk[:, o * T:(o + 1) * T], nf[:, o * T:(o + 1) * T], rb[:, :]),
                        reads=["b_nkvf", "rstd_bc"], writes=["b_nkvn"])

            for h in range(HB):
                def kn_evac(tc, bank, rb):
                    p.op("DVE", lambda v, tc=tc, bank=bank, sg=stg: v.tensor_copy(
                        sg[:, tc * 512:(tc + 1) * 512], bank[:, :]),
                        reads=[rb], writes=["b_stg"])
                bigproj(None, nkvn, "b_nkvn", 4, None, I["bWdkn"], colkv, "colkv",
                        kn_evac, h, scale_col=True)
                p.dma(kag_in[h * 192:h * 192 + 128, :], stg[:, :],
                      reads=["b_stg"], writes=[f"kag{h}"])

            with ExitStack() as s3b:
                kr64 = mk_sbuf(s3b, "b_kr64", [64, T], F32)
                tmpi = mk_sbuf(s3b, "b_tmpi", [32, T], F32)
                rc_ = mk_sbuf(s3b, "b_rc", [32, T], F32)
                ri_ = mk_sbuf(s3b, "b_ri", [32, T], F32)
                ro_ = mk_sbuf(s3b, "b_ro", [32, T], BF16)
                p.dma(wch[0][:, 0:16 * 64],
                      I["bWkr"].rearrange("(k p) o -> p k o", p=128),
                      writes=["b_wch0"], eng="POOL")
                for tc in range(4):
                    bank = PS[4 + (tc % 2)]
                    rb = f"ps{4 + (tc % 2)}"
                    fns = []
                    for k in range(ND):
                        fns.append(lambda t, k=k, tc=tc, bank=bank: t.matmul(
                            bank[0:64, :], wch[0][:, k * 64:(k + 1) * 64],
                            yTt(k)[:, tc * 512:(tc + 1) * 512],
                            start=(k == 0), stop=(k == ND - 1)))
                    p.group("PE", fns, reads=["b_wch0"] + [f"yT{k}" for k in range(ND)],
                            writes=[rb])
                    p.op("DVE", lambda v, tc=tc, bank=bank, kr=kr64: v.tensor_copy(
                        kr[:, tc * 512:(tc + 1) * 512], bank[0:64, :]),
                        reads=[rb], writes=["b_kr64"])
                p.op("DVE", lambda v, kr=kr64, ti=tmpi: v.tensor_copy(
                    ti[:, :], kr[32:64, :]), reads=["b_kr64"], writes=["b_tmpi"])
                for h in range(HB):
                    p.op("DVE", lambda v, h=h, kr=kr64, rc=rc_: v.tensor_scalar_mul(
                        rc[:, :], kr[0:32, :], fcT[:, h:h + 1]),
                        reads=["b_kr64", "fcT"], writes=["b_rc"])
                    p.op("DVE", lambda v, h=h, kr=kr64, ri=ri_: v.tensor_scalar_mul(
                        ri[:, :], kr[0:32, :], fsT[:, h:h + 1]),
                        reads=["b_kr64", "fsT"], writes=["b_ri"])
                    p.op("DVE", lambda v, h=h, ti=tmpi, rc=rc_, ro=ro_: v.scalar_tensor_tensor(
                        ro[:, :], ti[:, :], nfsT[:, h:h + 1], rc[:, :], ALU.mult, ALU.add),
                        reads=["b_tmpi", "nfsT", "b_rc"], writes=["b_ro"])
                    p.dma(kag_in[h * 192 + 128:h * 192 + 160, :], ro_[:, :],
                          reads=["b_ro"], writes=[f"kag{h}"])
                    p.op("DVE", lambda v, h=h, ti=tmpi, ri=ri_, ro=ro_: v.scalar_tensor_tensor(
                        ro[:, :], ti[:, :], fcT[:, h:h + 1], ri[:, :], ALU.mult, ALU.add),
                        reads=["b_tmpi", "fcT", "b_ri"], writes=["b_ro"])
                    p.dma(kag_in[h * 192 + 160:h * 192 + 192, :], ro_[:, :],
                          reads=["b_ro"], writes=[f"kag{h}"])

            p.barrier()
            nqb = mk_sbuf(s3, "b_nqb", [128, 12 * T], BF16)
            with ExitStack() as s3c:
                nq_f = mk_sbuf(s3c, "b_nqf", [128, T], F32)
                for o in range(12):
                    def nq_evac(tc, bank, rb):
                        p.op("DVE", lambda v, tc=tc, bank=bank, nf=nq_f: v.tensor_copy(
                            nf[:, tc * 512:(tc + 1) * 512], bank[:, :]),
                            reads=[rb], writes=["b_nqf"])
                    bigproj(None, None, None, ND, None, I["bWcq"], None, None,
                            nq_evac, o, scale_col=False)
                    p.op("ACT", lambda s, nf=nq_f, sq=sqb: s.activation(
                        sq[:, :], nf[:, :], AF.Square), reads=["b_nqf"], writes=["b_sqb"])
                    p.op("DVE", lambda v, o=o, nf=nq_f, nq=nqb: v.tensor_copy(
                        nq[:, o * T:(o + 1) * T], nf[:, :]), reads=["b_nqf"], writes=["b_nqb"])
                    for tc in range(4):
                        p.op("PE", lambda t, tc=tc, sq=sqb: t.matmul(
                            PS[3][0:1, :], onescolB[:, :], sq[:, tc * 512:(tc + 1) * 512],
                            start=True, stop=True),
                            reads=["b_sqb", "onescolB"], writes=["ps3"])
                        if o == 0:
                            p.op("DVE", lambda v, tc=tc, rw=rowA: v.tensor_copy(
                                rw[:, tc * 512:(tc + 1) * 512], PS[3][0:1, :]),
                                reads=["ps3"], writes=["rowA"])
                        else:
                            p.op("DVE", lambda v, tc=tc, rw=rowA: v.tensor_tensor(
                                rw[:, tc * 512:(tc + 1) * 512], rw[:, tc * 512:(tc + 1) * 512],
                                PS[3][0:1, :], ALU.add), reads=["ps3", "rowA"], writes=["rowA"])
                rows_finalize(R_Q)
                for o in range(12):
                    p.op("DVE", lambda v, o=o, nq=nqb, rb=rstd_bc: v.tensor_mul(
                        nq[:, o * T:(o + 1) * T], nq[:, o * T:(o + 1) * T], rb[:, :]),
                        reads=["b_nqb", "rstd_bc"], writes=["b_nqb"])

            for h in range(HB):
                def qn_evac(tc, bank, rb):
                    p.op("DVE", lambda v, tc=tc, bank=bank, sg=stg: v.tensor_copy(
                        sg[:, tc * 512:(tc + 1) * 512], bank[:, :]),
                        reads=[rb], writes=["b_stg"])
                bigproj(None, nqb, "b_nqb", 12, None, I["bWdqn"], colq, "colq",
                        qn_evac, h, scale_col=True)
                p.dma(qag_in[h * 192:h * 192 + 128, :], stg[:, :],
                      reads=["b_stg"], writes=[f"qag{h}"])

            with ExitStack() as s3d:
                qr_f = mk_sbuf(s3d, "b_qrf", [128, T], F32)
                tmpi = mk_sbuf(s3d, "b_qtmpi", [32, T], F32)
                rc_ = mk_sbuf(s3d, "b_qrc", [32, T], F32)
                ri_ = mk_sbuf(s3d, "b_qri", [32, T], F32)
                ro_ = mk_sbuf(s3d, "b_qro", [32, T], BF16)
                for o in range(8):
                    def qr_evac(tc, bank, rb):
                        p.op("DVE", lambda v, tc=tc, bank=bank, qf=qr_f: v.tensor_copy(
                            qf[:, tc * 512:(tc + 1) * 512], bank[:, :]),
                            reads=[rb], writes=["b_qrf"])
                    bigproj(None, nqb, "b_nqb", 12, None, I["bWdqr"], colq, "colq",
                            qr_evac, o, scale_col=True)
                    for sub in range(2):
                        h = 2 * o + sub
                        rb2 = sub * 64
                        p.op("DVE", lambda v, rb2=rb2, qf=qr_f, ti=tmpi: v.tensor_copy(
                            ti[:, :], qf[rb2 + 32:rb2 + 64, :]), reads=["b_qrf"], writes=["b_qtmpi"])
                        p.op("DVE", lambda v, h=h, rb2=rb2, qf=qr_f, rc=rc_: v.tensor_scalar_mul(
                            rc[:, :], qf[rb2:rb2 + 32, :], fcT[:, h:h + 1]),
                            reads=["b_qrf", "fcT"], writes=["b_qrc"])
                        p.op("DVE", lambda v, h=h, rb2=rb2, qf=qr_f, ri=ri_: v.tensor_scalar_mul(
                            ri[:, :], qf[rb2:rb2 + 32, :], fsT[:, h:h + 1]),
                            reads=["b_qrf", "fsT"], writes=["b_qri"])
                        p.op("DVE", lambda v, h=h, ti=tmpi, rc=rc_, ro=ro_: v.scalar_tensor_tensor(
                            ro[:, :], ti[:, :], nfsT[:, h:h + 1], rc[:, :], ALU.mult, ALU.add),
                            reads=["b_qtmpi", "nfsT", "b_qrc"], writes=["b_qro"])
                        p.dma(qag_in[h * 192 + 128:h * 192 + 160, :], ro_[:, :],
                              reads=["b_qro"], writes=[f"qag{h}"])
                        p.op("DVE", lambda v, h=h, ti=tmpi, ri=ri_, ro=ro_: v.scalar_tensor_tensor(
                            ro[:, :], ti[:, :], fcT[:, h:h + 1], ri[:, :], ALU.mult, ALU.add),
                            reads=["b_qtmpi", "fcT", "b_qri"], writes=["b_qro"])
                        p.dma(qag_in[h * 192 + 160:h * 192 + 192, :], ro_[:, :],
                              reads=["b_qro"], writes=[f"qag{h}"])

            p.barrier()
            s3v = s3.enter_context(ExitStack())
            wdv_sb = mk_sbuf(s3v, "b_wdv", [128, 4 * D], BF16)
            p.dma(wdv_sb[:, :], I["bWdv"].rearrange("(k p) o -> p k o", p=128),
                  writes=["b_wdv"], eng="POOL")
            for k in range(4):
                p.op("DVE", lambda v, k=k: v.tensor_scalar_mul(
                    wdv_sb[:, k * D:(k + 1) * D], wdv_sb[:, k * D:(k + 1) * D],
                    colkv[:, k:k + 1]), reads=["b_wdv", "colkv"], writes=["b_wdv"])
            VSC = 1.0 / float(np.sqrt(HB * D_V))
            for lt in range(NT):
                for dvc in range(4):
                    bank = PS[4 + (dvc % 2)]
                    rb = f"ps{4 + (dvc % 2)}"
                    fns = []
                    for k in range(4):
                        fns.append(lambda t, k=k, dvc=dvc, lt=lt, bank=bank: t.matmul(
                            bank[:, :], nkvn[:, k * T + lt * 128:k * T + (lt + 1) * 128],
                            wdv_sb[:, k * D + dvc * 512:k * D + (dvc + 1) * 512],
                            start=(k == 0), stop=(k == 3)))
                    p.group("PE", fns, reads=["b_nkvn", "b_wdv"], writes=[rb])
                    p.op("ACT", lambda s, dvc=dvc, bank=bank, sg=stg: s.activation(
                        sg[:, dvc * 512:(dvc + 1) * 512], bank[:, :], AF.Identity,
                        scale=VSC), reads=[rb], writes=["b_stg"])
                p.dma(vag_in[lt * 128:(lt + 1) * 128, :], stg[:, :],
                      reads=["b_stg"], writes=[f"vag{lt}"])
            p.barrier()

        p.collective("AllGather", ALU.bypass, [qag_in], [qag_out],
                     reads=[f"qag{h}" for h in range(HB)], writes=["qag_out"])
        p.collective("AllGather", ALU.bypass, [kag_in], [kag_out],
                     reads=[f"kag{h}" for h in range(HB)], writes=["kag_out"])
        p.collective("AllGather", ALU.bypass, [vag_in], [vag_out],
                     reads=[f"vag{lt}" for lt in range(NT)], writes=["vag_out"])

        with ExitStack() as s4:
            kA = mk_sbuf(s4, "at_kA", [128, S], BF16)
            kB = mk_sbuf(s4, "at_kB", [64, S], BF16)
            qA = mk_sbuf(s4, "at_qA", [128, S], BF16)
            qB = mk_sbuf(s4, "at_qB", [64, S], BF16)
            vSb = mk_sbuf(s4, "at_v", [128, S], BF16)
            Pb = mk_sbuf(s4, "at_P", [128, S], BF16)
            sq5 = mk_sbuf(s4, "at_sq", [128, 512], BF16)
            ptb = [mk_sbuf(s4, f"at_pt{i}", [128, 128], BF16) for i in range(2)]
            osg = mk_sbuf(s4, "at_osg", [128, 128], BF16)
            nqrow = mk_sbuf(s4, "at_nqrow", [1, S], F32)
            mrow = nqrow
            r512 = mk_sbuf(s4, "at_r512", [1, 512], F32)
            kmx = mk_sbuf(s4, "at_kmx", [1, 2], F32)
            mcol = mk_sbuf(s4, "at_mcol", [128, 2], F32)
            lcol = mk_sbuf(s4, "at_lcol", [128, 3], F32)

            CPQK = 4 * 192 * T
            CQQK = 1024 - HB * 192 * T
            CPV = 4 * 128
            CQV = 1024 * D - 16 * 128

            for prl in range(4):
                p.dma(kloc.rearrange("d (m c) -> d m c", m=8),
                      lambda e, c0=prl * 192 * T: bass.AP(
                          kag_out.tensor, p.emit_off(e, "POOL", c0, cp=CPQK, cq=CQQK),
                          [[T, 192], [HB * 192 * T, 8], [1, 1024]]),
                      reads=["kag_out"], writes=["kloc"], eng="POOL")
                p.dma(qloc.rearrange("d (m c) -> d m c", m=8),
                      lambda e, c0=prl * 192 * T: bass.AP(
                          qag_out.tensor, p.emit_off(e, "POOL", c0, cp=CPQK, cq=CQQK),
                          [[T, 192], [HB * 192 * T, 8], [1, 1024]]),
                      reads=["qag_out"], writes=["qloc"], eng="POOL")
                p.dma(lambda e: bass.AP(vloc.tensor, 0,
                                        [[128, 1024], [1024 * 128, 8], [1, 128]]),
                      lambda e, c0=prl * 128: bass.AP(
                          vag_out.tensor, p.emit_off(e, "POOL", c0, cp=CPV, cq=CQV),
                          [[D, 1024], [T * D, 8], [1, 128]]),
                      reads=["vag_out"], writes=["vloc"], eng="POOL")
                p.dma(kA[:, :], kloc[0:128, :], reads=["kloc"], writes=["at_kA"])
                p.dma(kB[:, :], kloc[128:192, :], reads=["kloc"], writes=["at_kB"])
                p.dma(qA[:, :], qloc[0:128, :], reads=["qloc"], writes=["at_qA"])
                p.dma(qB[:, :], qloc[128:192, :], reads=["qloc"], writes=["at_qB"])
                p.dma(vSb[:, :],
                      bass.AP(vloc.tensor, 0, [[128, 128], [128 * 128, 64], [1, 128]]),
                      reads=["vloc"], writes=["at_v"])

                for ch in range(16):
                    p.op("ACT", lambda s, ch=ch, sq=sq5, g=kA: s.activation(
                        sq[:, :], g[:, ch * 512:(ch + 1) * 512], AF.Square),
                        reads=["at_kA"], writes=["at_sq"])
                    p.op("PE", lambda t, sq=sq5: t.matmul(
                        PS[3][0:1, :], onescolB[:, :], sq[:, :],
                        start=True, stop=False), reads=["at_sq", "onescolB"], writes=["ps3"])
                    p.op("ACT", lambda s, ch=ch, sq=sq5, g=kB: s.activation(
                        sq[0:64, :], g[:, ch * 512:(ch + 1) * 512], AF.Square),
                        reads=["at_kB", "at_sq"], writes=["at_sq"])
                    p.op("PE", lambda t, sq=sq5: t.matmul(
                        PS[3][0:1, :], onescolB[0:64, :], sq[0:64, :],
                        start=False, stop=True), reads=["at_sq", "onescolB"], writes=["ps3"])
                    p.op("DVE", lambda v, r5=r512: v.tensor_copy(
                        r5[:, :], PS[3][0:1, :]), reads=["ps3"], writes=["at_r512"])
                    p.op("DVE", lambda v, r5=r512, ch=ch: v.reduce_max(
                        kmx[:, 1:2], r5[:, :], axis=AX.X), reads=["at_r512"], writes=["at_kmx"])
                    if ch == 0:
                        p.op("DVE", lambda v: v.tensor_copy(kmx[:, 0:1], kmx[:, 1:2]),
                             reads=["at_kmx"], writes=["at_kmx"])
                    else:
                        p.op("DVE", lambda v: v.tensor_tensor(
                            kmx[:, 0:1], kmx[:, 0:1], kmx[:, 1:2], ALU.max),
                            reads=["at_kmx"], writes=["at_kmx"])
                for ch in range(16):
                    p.op("ACT", lambda s, ch=ch, sq=sq5, g=qA: s.activation(
                        sq[:, :], g[:, ch * 512:(ch + 1) * 512], AF.Square),
                        reads=["at_qA"], writes=["at_sq"])
                    p.op("PE", lambda t, sq=sq5: t.matmul(
                        PS[3][0:1, :], onescolB[:, :], sq[:, :],
                        start=True, stop=False), reads=["at_sq", "onescolB"], writes=["ps3"])
                    p.op("ACT", lambda s, ch=ch, sq=sq5, g=qB: s.activation(
                        sq[0:64, :], g[:, ch * 512:(ch + 1) * 512], AF.Square),
                        reads=["at_qB", "at_sq"], writes=["at_sq"])
                    p.op("PE", lambda t, sq=sq5: t.matmul(
                        PS[3][0:1, :], onescolB[0:64, :], sq[0:64, :],
                        start=False, stop=True), reads=["at_sq", "onescolB"], writes=["ps3"])
                    p.op("DVE", lambda v, ch=ch, dr=nqrow: v.tensor_copy(
                        dr[:, ch * 512:(ch + 1) * 512], PS[3][0:1, :]),
                        reads=["ps3"], writes=["at_nrow"])
                p.op("DVE", lambda v: v.tensor_scalar(
                    mrow[:, :], nqrow[:, :], kmx[:, 0:1], 1.0 / 192.0, ALU.mult, ALU.mult),
                    reads=["at_nrow", "at_kmx"], writes=["at_nrow"])
                p.op("ACT", lambda s: s.activation(mrow[:, :], mrow[:, :], AF.Sqrt),
                     reads=["at_nrow"], writes=["at_nrow"])
                p.op("DVE", lambda v: v.tensor_scalar_mul(mrow[:, :], mrow[:, :], -1.0),
                     reads=["at_nrow"], writes=["at_nrow"])

                for qt in range(PT):
                    mq, aq = qt % 8, qt // 8
                    qcol = mq * 1024 + aq * 128
                    p.op("PE", lambda t, qcol=qcol: t.transpose(
                        PS[2][:, 0:1], mrow[:, qcol:qcol + 128], identF[0:1, 0:1]),
                        reads=["at_nrow", "identF"], writes=["ps2"])
                    p.op("DVE", lambda v: v.tensor_copy(mcol[:, 0:1], PS[2][:, 0:1]),
                         reads=["ps2"], writes=["at_mcol"])
                    runs = []
                    for mp in range(8):
                        full = (qt - mp + 7) // 8
                        c0 = mp * 1024
                        while full > 0:
                            w = min(full * 128, 512)
                            runs.append((c0, w, False))
                            c0 += w
                            full -= w // 128
                    runs.append((mq * 1024 + aq * 128, 128, True))
                    first_l = True
                    for (c0, w, is_diag) in runs:
                        bank = PS[4 + (len(runs) % 2) if False else 4 + (c0 // 512) % 2]
                        rb = f"ps{4 + (c0 // 512) % 2}"
                        fns = [
                            lambda t, c0=c0, w=w, qcol=qcol, bank=bank: t.matmul(
                                bank[:, 0:w], qA[:, qcol:qcol + 128], kA[:, c0:c0 + w],
                                start=True, stop=False),
                            lambda t, c0=c0, w=w, qcol=qcol, bank=bank: t.matmul(
                                bank[:, 0:w], qB[:, qcol:qcol + 128], kB[:, c0:c0 + w],
                                start=False, stop=True),
                        ]
                        p.group("PE", fns, reads=["at_qA", "at_qB", "at_kA", "at_kB"],
                                writes=[rb])
                        acc_ix = 2 if is_diag else 1
                        p.op("ACT", lambda s, c0=c0, w=w, bank=bank, ai=acc_ix: s.activation(
                            Pb[:, c0:c0 + w], bank[:, 0:w], AF.Exp,
                            bias=mcol[:, 0:1], scale=ATTN_SCALE,
                            accum_out=lcol[:, ai:ai + 1]),
                            reads=[rb, "at_mcol"], writes=["at_P", "at_lc"])
                        if is_diag:
                            p.op("DVE", lambda v, c0=c0: v.tensor_mul(
                                Pb[:, c0:c0 + 128], Pb[:, c0:c0 + 128], trilB[:, :]),
                                reads=["at_P", "tril"], writes=["at_P"])
                            p.op("DVE", lambda v, c0=c0: v.reduce_sum(
                                lcol[:, 2:3], Pb[:, c0:c0 + 128], axis=AX.X),
                                reads=["at_P"], writes=["at_lc"])
                            if first_l:
                                p.op("DVE", lambda v: v.tensor_copy(
                                    lcol[:, 0:1], lcol[:, 2:3]),
                                    reads=["at_lc"], writes=["at_lc"])
                                first_l = False
                            else:
                                p.op("DVE", lambda v: v.tensor_add(
                                    lcol[:, 0:1], lcol[:, 0:1], lcol[:, 2:3]),
                                    reads=["at_lc"], writes=["at_lc"])
                        elif first_l:
                            p.op("DVE", lambda v: v.tensor_copy(lcol[:, 0:1], lcol[:, 1:2]),
                                 reads=["at_lc"], writes=["at_lc"])
                            first_l = False
                        else:
                            p.op("DVE", lambda v: v.tensor_add(
                                lcol[:, 0:1], lcol[:, 0:1], lcol[:, 1:2]),
                                reads=["at_lc"], writes=["at_lc"])
                    p.op("DVE", lambda v: v.reciprocal(lcol[:, 1:2], lcol[:, 0:1]),
                         reads=["at_lc"], writes=["at_lc"])
                    for (c0, w, is_diag) in runs:
                        p.op("DVE", lambda v, c0=c0, w=w: v.tensor_scalar_mul(
                            Pb[:, c0:c0 + w], Pb[:, c0:c0 + w], lcol[:, 1:2]),
                            reads=["at_P", "at_lc"], writes=["at_P"])
                    taus = []
                    for (c0, w, is_diag) in runs:
                        for cc in range(c0, c0 + w, 128):
                            taus.append(cc // 128)
                    for ti_, tau in enumerate(taus):
                        bank = PB[ti_ % 2]
                        rpb = f"pb{ti_ % 2}"
                        p.op("PE", lambda t, tau=tau, bank=bank: t.transpose(
                            bank[:, 0:128], Pb[:, tau * 128:(tau + 1) * 128], identB[:, :]),
                            reads=["at_P", "identB"], writes=[rpb])
                        p.op("DVE", lambda v, ti_=ti_, bank=bank, pp=ptb[ti_ % 2]: v.tensor_copy(
                            pp[:, :], bank[:, 0:128]), reads=[rpb], writes=[f"at_ptb{ti_ % 2}"])
                        p.op("PE", lambda t, tau=tau, ti_=ti_, pp=ptb[ti_ % 2], n=len(taus): t.matmul(
                            PS[0][:, 0:128], vSb[:, tau * 128:(tau + 1) * 128], pp[:, :],
                            start=(ti_ == 0), stop=(ti_ == n - 1)),
                            reads=[f"at_ptb{ti_ % 2}", "at_v"], writes=["ps0"])
                    p.op("ACT", lambda s: s.copy(osg[:, :], PS[0][:, 0:128]),
                         reads=["ps0"], writes=["at_osg"])
                    p.dma(oag_in[prl * 128:(prl + 1) * 128, qt * 128:(qt + 1) * 128],
                          osg[:, :], reads=["at_osg"], writes=[f"oag{prl}_{qt}"])
            p.barrier()

        p.collective("AllGather", ALU.bypass, [oag_in], [oag_out],
                     reads=[f"oag{prl}_{qt}" for prl in range(4) for qt in range(PT)],
                     writes=["oag_out"])

        for b2 in range(2):
            p.dma(oloc.rearrange("d (k i) -> d k i", k=16)[:, 8 * b2:8 * b2 + 8, :],
                  lambda e, b2=b2: bass.AP(
                      oag_out.tensor, p.emit_off(e, "SP", b2 * 2048 * S, cp=128),
                      [[S, 2048], [1024, 8], [1, 128]]),
                  reads=["oag_out"], writes=["oloc"], eng="SP")

        stW2 = stack.enter_context(ExitStack())
        cb2 = mk_sbuf(stW2, "cb2", [128, ND * T], BF16)
        with ExitStack() as s5:
            TH = T // 2
            oT_sb = mk_sbuf(s5, "oT_sb", [128, ND * TH], BF16)
            wpb = [mk_sbuf(s5, f"wpb{i}", [128, 16 * 128], BF16) for i in range(2)]
            for th in range(2):
                for k in range(ND):
                    p.dma(oT_sb[:, k * TH:(k + 1) * TH],
                          oloc[k * 128:(k + 1) * 128, th * TH:(th + 1) * TH],
                          reads=["oloc"], writes=[f"oTs{k}"])
                for o in range(ND):
                    wi = o % 2
                    p.dma(wpb[wi][:, :],
                          I["bWproj"].rearrange("(k p) o -> p k o", p=128)[:, :, o * 128:(o + 1) * 128],
                          writes=[f"wpb{wi}"], eng="POOL")
                    for tc in range(2):
                        bank = PS[4 + (tc % 2)]
                        rb = f"ps{4 + (tc % 2)}"
                        fns = []
                        for k in range(ND):
                            fns.append(lambda t, k=k, tc=tc, wb=wpb[wi], bank=bank, ot=oT_sb: t.matmul(
                                bank[:, :], wb[:, k * 128:(k + 1) * 128],
                                ot[:, k * TH + tc * 512:k * TH + (tc + 1) * 512],
                                start=(k == 0), stop=(k == ND - 1)))
                        p.group("PE", fns,
                                reads=[f"wpb{wi}"] + [f"oTs{k}" for k in range(ND)],
                                writes=[rb])
                        p.op("ACT", lambda s, tc=tc, o=o, th=th, bank=bank, cb=cb2: s.copy(
                            cb[:, o * T + th * TH + tc * 512:o * T + th * TH + (tc + 1) * 512],
                            bank[:, :]),
                            reads=[rb], writes=[f"cb2_{o}"])
            p.barrier()

        with ExitStack() as st3:
            emit_boundary("b3", xr2, cb2, "cb2_", xr1, "xr1", st3, do_ln=True,
                          src_region="xr2")
            p.barrier()
        stW2.close()

        with ExitStack() as st4:
            emit_ffn("f2", xr1, "xr1", out, "out", st4, do_ln=False)

        with ExitStack() as stZ:
            if debug_dump is not None:
                for k in range(ND):
                    p.dma(dbg[k * 128:(k + 1) * 128, :], yTt(k), reads=[f"yT{k}"],
                          writes=[f"dbg{k}"], eng="POOL")
            p.barrier()
            p.drain_dmas(eng="SP")
        p.finalize()

    return nc


def kernel(**inputs):
    in_maps = host_inputs(inputs)
    nc = build_nc()
    res = run_bass_kernel_spmd(nc, in_maps, core_ids=list(range(NCORE)))
    o4 = np.zeros((B, PT, 128, D), dtype=np.float32)
    for c in range(NCORE):
        shard = np.asarray(res.results[c]["out_shard"], dtype=np.float32).reshape(NT, 128, D)
        tiles = owner_tiles(c)
        i = 0
        for b in range(B):
            for pt in tiles:
                o4[b, pt] = shard[i]
                i += 1
    return o4.reshape(B, S, D)




# version 04
# speedup vs baseline: 1.0808x; 1.0808x over previous
import numpy as np
import ml_dtypes
from contextlib import ExitStack

import concourse.bass as bass
import concourse.mybir as mybir
from concourse.bass_utils import run_bass_kernel_spmd

F32 = mybir.dt.float32
BF16 = mybir.dt.bfloat16
I32 = mybir.dt.int32
AF = mybir.ActivationFunctionType
ALU = mybir.AluOpType
AX = mybir.AxisListType

NCORE = 8
B = 2
S = 8192
D = 2048
T = 2048
NT = 16
ND = 16
PT = 64
EPS_LN = 1e-5

HA = 16; HKV = 8; DH = 128
HB = 16; D_NOPE = 128; D_ROPE = 64; D_V = 128; R_Q = 1536; R_KV = 512
ATTN_SCALE = 1.0 / float(np.sqrt(192.0))
NE = 32; INTER = 768; TOPK = 8; TAU = 0.05

BF = ml_dtypes.bfloat16
INV2PI = 1.0 / (2.0 * np.pi)
TWOPI = 2.0 * np.pi


def owner_tiles(c):
    return [c + 8 * k for k in range(8)]



class Tok:
    __slots__ = ("sem", "val")
    def __init__(self, sem, val):
        self.sem = sem
        self.val = val


ENGS = ("PE", "ACT", "DVE", "POOL", "SP")


class Prog:
    def __init__(self, nc, stack, n_spd=10, n_swd=6):
        self.nc = nc
        self.selfsem = {e: stack.enter_context(nc.semaphore(f"S{e}")) for e in ENGS}
        self.count = {e: 0 for e in ENGS}
        self.clock = {e: {} for e in ENGS}
        self.writer = {}
        self.readers = {}
        self.spd = [stack.enter_context(nc.semaphore(f"spd{i}")) for i in range(n_spd)]
        self.spd_count = [0] * n_spd
        self.spd_next = 0
        self.swd = [stack.enter_context(nc.semaphore(f"swd{i}")) for i in range(n_swd)]
        self.swd_count = [0] * n_swd
        self.swd_next = 0
        self.ccsem = stack.enter_context(nc.semaphore("cc"))
        self.cc_count = 0
        self.blocks = [{e: [] for e in ENGS}]
        self.n_inst = 0
        self.dynst = {}

    def emit_off(self, e, q, const, cp=0, cq=0, cm=0):
        st = self.dynst.get(q)
        if st is None:
            pid = e.partition_id()
            r_pid = e.to_reg(pid)
            r_q = e.alloc_register(f"{q}_dq")
            r_m = e.alloc_register(f"{q}_dm")
            r_off = e.alloc_register(f"{q}_do")
            r_t = e.alloc_register(f"{q}_dt")
            e.reg_div(r_q, r_pid, 4)
            e.reg_mod(r_m, r_pid, 4)
            st = {"pid": r_pid, "q": r_q, "m": r_m, "off": r_off, "t": r_t}
            self.dynst[q] = st
        e.reg_mov(st["off"], const)
        for coef, rsrc in ((cp, st["pid"]), (cq, st["q"]), (cm, st["m"])):
            if coef:
                e.reg_mul(st["t"], rsrc, coef)
                e.reg_add(st["off"], st["off"], st["t"])
        return st["off"]

    def _deps(self, reads, writes):
        toks = []
        for r in reads:
            w = self.writer.get(r)
            if w is not None:
                toks.append(w)
        for w in writes:
            ww = self.writer.get(w)
            if ww is not None:
                toks.append(ww)
            toks.extend(self.readers.get(w, ()))
        return toks

    def _waits_for(self, eng, toks, pe_exempt=False):
        m = {}
        for t in toks:
            if pe_exempt and eng == "PE" and t.sem is self.selfsem["PE"]:
                continue
            k = t.sem
            if m.get(k, 0) < t.val:
                m[k] = t.val
        out = []
        clock = self.clock[eng]
        for k, v in m.items():
            if clock.get(k, 0) < v:
                out.append((k, v))
                clock[k] = v
        return out

    def _register(self, tok, reads, writes):
        for r in reads:
            self.readers.setdefault(r, []).append(tok)
        for w in writes:
            self.writer[w] = tok
            self.readers[w] = []

    def op(self, eng, fn, reads=(), writes=()):
        waits = self._waits_for(eng, self._deps(reads, writes), pe_exempt=True)
        self.count[eng] += 1
        tok = Tok(self.selfsem[eng], self.count[eng])
        self.blocks[-1][eng].append((waits, fn, [(tok.sem, 1)]))
        self._register(tok, reads, writes)
        self.n_inst += 1
        return tok

    def group(self, eng, fns, reads=(), writes=()):
        waits = self._waits_for(eng, self._deps(reads, writes), pe_exempt=True)
        self.count[eng] += 1
        tok = Tok(self.selfsem[eng], self.count[eng])
        items = self.blocks[-1][eng]
        for i, fn in enumerate(fns):
            if i == 0:
                items.append((waits, fn, [] if len(fns) > 1 else [(tok.sem, 1)]))
            elif i == len(fns) - 1:
                items.append(([], fn, [(tok.sem, 1)]))
            else:
                items.append(([], fn, []))
        self._register(tok, reads, writes)
        self.n_inst += len(fns)
        return tok

    def dma(self, out_ap, in_ap, reads=(), writes=(), eng="SP", **kw):
        deps = self._deps(reads, writes)
        if eng == "SP":
            q, sems, counts = "SP", self.spd, self.spd_count
            i = self.spd_next
            self.spd_next = (self.spd_next + 1) % len(sems)
        else:
            q, sems, counts = "POOL", self.swd, self.swd_count
            i = self.swd_next
            self.swd_next = (self.swd_next + 1) % len(sems)
        sem = sems[i]
        if counts[i] > 0:
            deps = list(deps) + [Tok(sem, counts[i])]
        waits = self._waits_for(q, deps)
        counts[i] += 16

        def fn(e, out_ap=out_ap, in_ap=in_ap, kw=kw):
            o = out_ap(e) if callable(out_ap) else out_ap
            ii = in_ap(e) if callable(in_ap) else in_ap
            return e.dma_start(o, ii, **kw)
        tok = Tok(sem, counts[i])
        self.blocks[-1][q].append((waits, fn, [(sem, 16)]))
        self._register(tok, reads, writes)
        self.n_inst += 1
        return tok

    def collective(self, kind, op, ins, outs, reads=(), writes=()):
        waits = self._waits_for("POOL", self._deps(reads, writes))
        self.cc_count += 1
        tok = Tok(self.ccsem, self.cc_count)
        rg = [list(range(NCORE))]
        self.blocks[-1]["POOL"].append(
            (waits,
             lambda e: e.collective_compute(kind, op, replica_groups=rg,
                                            ins=[a.opt() for a in ins],
                                            outs=[a.opt() for a in outs]),
             [(self.ccsem, 1)]))
        self._register(tok, reads, writes)
        self.n_inst += 1
        return tok

    def barrier(self):
        toks = [Tok(self.selfsem[e], self.count[e]) for e in ENGS if self.count[e]]
        toks += [Tok(self.spd[i], self.spd_count[i])
                 for i in range(len(self.spd)) if self.spd_count[i]]
        toks += [Tok(self.swd[i], self.swd_count[i])
                 for i in range(len(self.swd)) if self.swd_count[i]]
        if self.cc_count:
            toks.append(Tok(self.ccsem, self.cc_count))
        for e in ENGS:
            waits = self._waits_for(e, toks)
            if waits:
                self.blocks[-1][e].append((waits, None, []))
        self.blocks.append({e: [] for e in ENGS})

    def drain_dmas(self, eng="SP"):
        toks = [Tok(self.spd[i], self.spd_count[i])
                for i in range(len(self.spd)) if self.spd_count[i]]
        toks += [Tok(self.swd[i], self.swd_count[i])
                 for i in range(len(self.swd)) if self.swd_count[i]]
        if self.cc_count:
            toks.append(Tok(self.ccsem, self.cc_count))
        waits = self._waits_for(eng, toks)
        if waits:
            self.blocks[-1][eng].append((waits, None, []))

    def finalize(self):
        nc = self.nc
        attr = {"PE": "tensor", "ACT": "scalar", "DVE": "vector",
                "POOL": "gpsimd", "SP": "sync"}
        for blk in self.blocks:
            if not any(blk[e] for e in ENGS):
                continue
            with nc.Block() as b:
                for e in ENGS:
                    items = blk[e]
                    if not items:
                        continue
                    def make(items):
                        def body(eng):
                            for waits, fn, incs in items:
                                for (sem, v) in waits:
                                    eng.wait_ge(sem, v)
                                if fn is None:
                                    continue
                                inst = fn(eng)
                                for (sem, n) in incs:
                                    inst.then_inc(sem, n)
                        return body
                    getattr(b, attr[e])(make(items))



def rope_tables():
    inv = 1.0 / (10000.0 ** (np.arange(0, DH, 2, dtype=np.float64) / DH))
    pos = np.arange(S, dtype=np.float64)
    fr = pos[:, None] * inv[None, :]
    emb = np.concatenate([fr, fr], axis=1)
    return (np.cos(emb).astype(np.float32).T.copy(),
            np.sin(emb).astype(np.float32).T.copy())


def perm_rope_cols(n_heads):
    cols = []
    for h in range(n_heads):
        cols.extend([h * 64 + 2 * k for k in range(32)])
        cols.extend([h * 64 + 2 * k + 1 for k in range(32)])
    return np.array(cols)


def core_offsets(c):
    off = np.zeros(64, dtype=np.int64)
    for ltp in range(16):
        b, k = ltp // 8, ltp % 8
        h = 2 * k + c // 4
        off[ltp] = (h * 128 + (2 * (c % 4) + 8 * b) * 8) * D
    off[16] = c * 128
    off[17] = c
    return off.astype(np.int32)


def host_inputs(inp):
    f32 = np.float32
    x = np.ascontiguousarray(inp["L0_attn_a_hidden_states"], dtype=f32)
    x4 = x.reshape(B, PT, 128, D)
    cos_t, sin_t = rope_tables()

    shared = {}
    shared["aWqT"] = np.ascontiguousarray(inp["L0_attn_a_Wq"].T, f32)
    shared["aWkT"] = np.ascontiguousarray(inp["L0_attn_a_Wk"].T, f32)
    shared["aWvT"] = np.ascontiguousarray(inp["L0_attn_a_Wv"].T, f32)
    shared["aWoT"] = np.ascontiguousarray(inp["L0_attn_a_Wo"].T, f32)
    shared["abq"] = np.ascontiguousarray(inp["L0_attn_a_bq"], f32)
    shared["abk"] = np.ascontiguousarray(inp["L0_attn_a_bk"], f32)
    shared["abv"] = np.ascontiguousarray(inp["L0_attn_a_bv"], f32)
    shared["abo"] = np.ascontiguousarray(inp["L0_attn_a_bo"], f32)
    shared["aRT"] = np.ascontiguousarray(inp["L0_attn_a_rand_matrix"].T, f32)
    shared["bWcq"] = np.ascontiguousarray(inp["L0_attn_b_w_cq"], f32)
    shared["bqnw"] = np.ascontiguousarray(inp["L0_attn_b_q_norm_w"], f32)
    shared["bWdqn"] = np.ascontiguousarray(inp["L0_attn_b_w_dq_nope"], f32)
    shared["bWdqr"] = np.ascontiguousarray(inp["L0_attn_b_w_dq_rope"][:, perm_rope_cols(HB)], f32)
    shared["bWckv"] = np.ascontiguousarray(inp["L0_attn_b_w_ckv"], f32)
    shared["bkvnw"] = np.ascontiguousarray(inp["L0_attn_b_kv_norm_w"], f32)
    shared["bWdkn"] = np.ascontiguousarray(inp["L0_attn_b_w_dk_nope"], f32)
    shared["bWdv"] = np.ascontiguousarray(inp["L0_attn_b_w_dv"], f32)
    shared["bWkr"] = np.ascontiguousarray(inp["L0_attn_b_w_k_rope"][:, perm_rope_cols(1)], f32)
    shared["bWproj"] = np.ascontiguousarray(inp["L0_attn_b_w_proj"], f32)
    shared["bfc"] = np.ascontiguousarray(inp["L0_attn_b_freqs_cos"], f32)
    shared["bfs"] = np.ascontiguousarray(inp["L0_attn_b_freqs_sin"], f32)
    shared["mWgT"] = np.ascontiguousarray(inp["L0_ffn_gate_weight"].T, f32)
    shared["mWguT"] = np.ascontiguousarray(inp["L0_ffn_gate_up_proj"].transpose(0, 2, 1), f32)
    shared["mWdT"] = np.ascontiguousarray(inp["L0_ffn_down_proj"].transpose(0, 2, 1), f32)
    shared["malpha"] = np.ascontiguousarray(np.asarray(inp["L0_ffn_layer_alpha"]).reshape(1), f32)
    shared["identF"] = np.eye(128, dtype=f32)
    shared["identB"] = np.eye(128).astype(BF)
    shared["tril"] = np.tril(np.ones((128, 128))).astype(BF)
    oh = np.zeros((NE, 32, 128), dtype=f32)
    for e in range(NE):
        oh[e, e, :] = 1.0
    shared["onehot"] = oh.astype(BF)

    in_maps = []
    for c in range(NCORE):
        tiles = owner_tiles(c)
        xs = np.concatenate([x4[b, p] for b in range(B) for p in tiles], axis=0)
        cs = np.concatenate([cos_t[:, p * 128:(p + 1) * 128] for b in range(B) for p in tiles], axis=1)
        ss = np.concatenate([sin_t[:, p * 128:(p + 1) * 128] for b in range(B) for p in tiles], axis=1)
        m = dict(shared)
        m["x"] = np.ascontiguousarray(xs)
        m["ropec"] = np.ascontiguousarray(cs)
        m["ropes"] = np.ascontiguousarray(ss)
        m["coff"] = core_offsets(c).reshape(1, 64)
        in_maps.append(m)
    return in_maps



def build_nc(debug_dump=None):
    nc = bass.Bass("TRN2", target_bir_lowering=False)

    I = {}
    def din(name, shape, dtype=F32):
        I[name] = nc.dram_tensor(name, list(shape), dtype, kind="ExternalInput").ap()
    din("x", (T, D))
    din("ropec", (128, T)); din("ropes", (128, T))
    din("aWqT", (D, D)); din("aWkT", (D, 1024)); din("aWvT", (D, 1024)); din("aWoT", (D, D))
    din("abq", (D,)); din("abk", (1024,)); din("abv", (1024,)); din("abo", (D,))
    din("aRT", (128, 64))
    din("bWcq", (D, R_Q)); din("bqnw", (R_Q,))
    din("bWdqn", (R_Q, D)); din("bWdqr", (R_Q, 1024))
    din("bWckv", (D, R_KV)); din("bkvnw", (R_KV,))
    din("bWdkn", (R_KV, D)); din("bWdv", (R_KV, D)); din("bWkr", (D, 64))
    din("bWproj", (D, D)); din("bfc", (HB, 32)); din("bfs", (HB, 32))
    din("mWgT", (D, NE)); din("mWguT", (NE, D, 2 * INTER)); din("mWdT", (NE, INTER, D))
    din("malpha", (1,))
    din("identF", (128, 128)); din("identB", (128, 128), BF16); din("tril", (128, 128), BF16)
    din("coff", (1, 64), I32)
    din("onehot", (NE, 32, 128), BF16)

    out = nc.dram_tensor("out_shard", [T, D], F32, kind="ExternalOutput").ap()
    dbg = None
    if debug_dump is not None:
        dbg = nc.dram_tensor("dbg", list(debug_dump), F32, kind="ExternalOutput").ap()

    xr1 = nc.dram_tensor("xr1", [T, D], F32, kind="Internal").ap()
    xr2 = nc.dram_tensor("xr2", [T, D], F32, kind="Internal").ap()
    q_shard = nc.dram_tensor("q_shard", [HB * 192, T], BF16, kind="Internal").ap()
    qag_in = nc.dram_tensor("qag_in", [HB * 192, T], BF16, kind="Internal").ap()
    qag_out = nc.dram_tensor("qag_out", [NCORE * HB * 192, T], BF16, kind="Internal").ap()
    kag_in = nc.dram_tensor("kag_in", [HB * 192, T], BF16, kind="Internal").ap()
    kag_out = nc.dram_tensor("kag_out", [NCORE * HB * 192, T], BF16, kind="Internal").ap()
    vag_in = nc.dram_tensor("vag_in", [T, D], BF16, kind="Internal").ap()
    vag_out = nc.dram_tensor("vag_out", [NCORE * T, D], BF16, kind="Internal").ap()
    oag_in = nc.dram_tensor("oag_in", [4 * 128, S], BF16, kind="Internal").ap()
    oag_out = nc.dram_tensor("oag_out", [NCORE * 4 * 128, S], BF16, kind="Internal").ap()
    qloc = nc.dram_tensor("qloc", [192, S], BF16, kind="Internal").ap()
    kloc = nc.dram_tensor("kloc", [192, S], BF16, kind="Internal").ap()
    vloc = nc.dram_tensor("vloc", [S, 128], BF16, kind="Internal").ap()
    oloc = nc.dram_tensor("oloc", [D, T], BF16, kind="Internal").ap()
    aag_in = nc.dram_tensor("aag_in", [T, D], BF16, kind="Internal").ap()
    aag_out = nc.dram_tensor("aag_out", [NCORE * T, D], BF16, kind="Internal").ap()
    wguC = nc.dram_tensor("wguC", [NE * 12 * 128, ND * 128], BF16, kind="Internal").ap()
    wdC = nc.dram_tensor("wdC", [NE * 4 * 128, 6 * 512], BF16, kind="Internal").ap()

    with ExitStack() as stack:
        def mk_sbuf(stk, name, shape, dtype):
            return stk.enter_context(nc.sbuf_tensor(name, list(shape), dtype))
        def psum(name, shape, dtype):
            return stack.enter_context(nc.psum_tensor(name, list(shape), dtype))

        p = Prog(nc, stack)
        PS = [psum(f"ps{i}", [128, 512], F32) for i in range(6)]
        PB = [psum(f"pb{i}", [128, 1024], BF16) for i in range(2)]

        def sbuf(name, shape, dtype):
            return mk_sbuf(stack, name, shape, dtype)

        identF = sbuf("identF_sb", [128, 128], F32)
        identB = sbuf("identB_sb", [128, 128], BF16)
        trilB = sbuf("tril_sb", [128, 128], BF16)
        ones1F = sbuf("ones1F", [1, 128], F32)
        onescolB = sbuf("onescolB", [128, 1], BF16)
        ones2F = sbuf("ones2F", [128, 128], BF16)
        offtab = sbuf("offtab", [1, 64], I32)
        p.dma(identF[:, :], I["identF"], writes=["identF"])
        p.dma(identB[:, :], I["identB"], writes=["identB"])
        p.dma(trilB[:, :], I["tril"], writes=["tril"])
        p.dma(offtab[:, :], I["coff"], writes=["offtab"])
        p.op("DVE", lambda v: v.memset(ones1F[:, :], 1.0), writes=["ones1F"])
        p.op("DVE", lambda v: v.memset(onescolB[:, :], 1.0), writes=["onescolB"])
        p.op("DVE", lambda v: v.memset(ones2F[:, :], 2.0 / 128.0), writes=["ones2F"])

        yT = sbuf("yT", [128, ND * T], BF16)
        def yTt(k):
            return yT[:, k * T:(k + 1) * T]

        bq_sb = sbuf("bq_sb", [128, 16], F32)
        bk_sb = sbuf("bk_sb", [128, 8], F32)
        bv_sb = sbuf("bv_sb", [128, 8], F32)
        bo_sb = sbuf("bo_sb", [128, 16], F32)
        for nm, tt, t_ in (("abq", bq_sb, "bq_sb"), ("abk", bk_sb, "bk_sb"),
                           ("abv", bv_sb, "bv_sb"), ("abo", bo_sb, "bo_sb")):
            p.dma(tt[:, :], I[nm].rearrange("(o p) -> p o", p=128),
                  writes=[t_], allow_slow_non_contiguous=True)
        alpha1 = sbuf("alpha1", [1, 1], F32)
        alpha_col = sbuf("alpha_col", [128, 1], F32)
        p.dma(alpha1[:, :], I["malpha"].rearrange("(a b) -> a b", a=1), writes=["alpha1"])
        p.op("PE", lambda t: t.matmul(PS[2][:, 0:1], ones1F[:, :], alpha1[:, :],
                                      start=True, stop=True),
             reads=["alpha1", "ones1F"], writes=["ps2"])
        p.op("DVE", lambda v: v.tensor_copy(alpha_col[:, :], PS[2][:, 0:1]),
             reads=["ps2"], writes=["alpha_col"])
        mWgT_sb = sbuf("mWgT_sb", [128, 16 * NE], BF16)
        p.dma(mWgT_sb[:, :], I["mWgT"].rearrange("(k p) e -> p k e", p=128),
              writes=["mWgT"], eng="POOL")
        oneh_sb = sbuf("oneh_sb", [32, NE * 128], BF16)
        p.dma(oneh_sb[:, :], I["onehot"].rearrange("e p f -> p e f"),
              writes=["oneh"], allow_slow_non_contiguous=True)

        def emit_boundary(tag, src_rows, cbuf, cb_region, dst_rows, dst_region,
                          st, do_ln=True, lts=None, cb_w=T, cb_lt0=0, cb_f32=False,
                          src_region=None):
            xrow = mk_sbuf(st, f"{tag}_xrow", [128, D], F32)
            crow = mk_sbuf(st, f"{tag}_crow", [128, D], F32)
            yrow = mk_sbuf(st, f"{tag}_yrow", [128, D], BF16)
            sq = yrow
            colA = mk_sbuf(st, f"{tag}_colA", [128, 8], F32)
            if lts is None:
                lts = range(NT)
            for lt in lts:
                p.dma(xrow[:, :], src_rows[lt * 128:(lt + 1) * 128, :],
                      reads=[f"{src_region or tag + '_srcrows'}{lt}"], writes=[f"{tag}_xrow"])
                if cbuf is not None:
                    for o in range(ND):
                        c0 = o * cb_w + (lt - cb_lt0) * 128
                        if cb_f32:
                            bank = PS[o % 2]
                            rpb = f"ps{o % 2}"
                            p.op("PE", lambda t, c0=c0, bank=bank, cb=cbuf: t.transpose(
                                bank[:, 0:128], cb[:, c0:c0 + 128], identF[:, :]),
                                reads=[f"{cb_region}{o}", "identF"], writes=[rpb])
                        else:
                            bank = PB[o % 2]
                            rpb = f"pb{o % 2}"
                            p.op("PE", lambda t, c0=c0, bank=bank, cb=cbuf: t.transpose(
                                bank[:, 0:128], cb[:, c0:c0 + 128], identB[:, :]),
                                reads=[f"{cb_region}{o}", "identB"], writes=[rpb])
                        p.op("DVE", lambda v, o=o, bank=bank, cr=crow: v.tensor_copy(
                            cr[:, o * 128:(o + 1) * 128], bank[:, 0:128]),
                            reads=[rpb], writes=[f"{tag}_crow"])
                    p.op("DVE", lambda v, xr=xrow, cr=crow: v.tensor_add(
                        xr[:, :], xr[:, :], cr[:, :]),
                        reads=[f"{tag}_xrow", f"{tag}_crow"], writes=[f"{tag}_xrow"])
                p.dma(dst_rows[lt * 128:(lt + 1) * 128, :], xrow[:, :],
                      reads=[f"{tag}_xrow"], writes=[f"{dst_region}{lt}"])
                if not do_ln:
                    continue
                p.op("DVE", lambda v, xr=xrow, ca=colA: v.reduce_sum(
                    ca[:, 5:6], xr[:, :], axis=AX.X),
                    reads=[f"{tag}_xrow"], writes=[f"{tag}_cA"])
                p.op("DVE", lambda v, xr=xrow, sq=sq, ca=colA: v.scalar_tensor_tensor(
                    sq[:, :], xr[:, :], 1.0, xr[:, :], ALU.mult, ALU.mult,
                    accum_out=ca[:, 0:1]), reads=[f"{tag}_xrow"],
                    writes=[f"{tag}_yrow", f"{tag}_cA"])
                p.op("DVE", lambda v, ca=colA: v.tensor_scalar_mul(
                    ca[:, 5:6], ca[:, 5:6], 1.0 / D), reads=[f"{tag}_cA"], writes=[f"{tag}_cA"])
                p.op("DVE", lambda v, ca=colA: v.scalar_tensor_tensor(
                    ca[:, 1:2], ca[:, 5:6], -1.0, ca[:, 5:6], ALU.mult, ALU.mult),
                    reads=[f"{tag}_cA"], writes=[f"{tag}_cA"])
                p.op("DVE", lambda v, ca=colA: v.tensor_scalar(
                    ca[:, 2:3], ca[:, 0:1], 1.0 / D, EPS_LN, ALU.mult, ALU.add),
                    reads=[f"{tag}_cA"], writes=[f"{tag}_cA"])
                p.op("DVE", lambda v, ca=colA: v.tensor_tensor(
                    ca[:, 3:4], ca[:, 2:3], ca[:, 1:2], ALU.add),
                    reads=[f"{tag}_cA"], writes=[f"{tag}_cA"])
                p.op("ACT", lambda s, ca=colA: s.activation(
                    ca[:, 4:5], ca[:, 3:4], AF.Sqrt),
                    reads=[f"{tag}_cA"], writes=[f"{tag}_cA"])
                p.op("DVE", lambda v, ca=colA: v.reciprocal(ca[:, 6:7], ca[:, 4:5]),
                     reads=[f"{tag}_cA"], writes=[f"{tag}_cA"])
                p.op("DVE", lambda v, xr=xrow, yr=yrow, ca=colA: v.tensor_scalar(
                    yr[:, :], xr[:, :], ca[:, 5:6], ca[:, 6:7], ALU.subtract, ALU.mult),
                    reads=[f"{tag}_xrow", f"{tag}_cA"], writes=[f"{tag}_yrow"])
                for o in range(ND):
                    bank = PB[o % 2]
                    p.op("PE", lambda t, o=o, bank=bank, yr=yrow: t.transpose(
                        bank[:, 0:128], yr[:, o * 128:(o + 1) * 128], identB[:, :]),
                        reads=[f"{tag}_yrow", "identB"], writes=[f"pb{o % 2}"])
                    p.op("DVE", lambda v, o=o, lt=lt, bank=bank: v.tensor_copy(
                        yTt(o)[:, lt * 128:(lt + 1) * 128], bank[:, 0:128]),
                        reads=[f"pb{o % 2}"], writes=[f"yT{o}"])

        with ExitStack() as st0:
            emit_boundary("p0", I["x"], None, None, xr2, "xjunk", st0, do_ln=True)
            p.barrier()

        with ExitStack() as stA:
            kp_sb = mk_sbuf(stA, "kp_sb", [128, HKV * T], BF16)
            v_sb = mk_sbuf(stA, "v_sb", [128, HKV * T], BF16)

            with ExitStack() as stP:
                cosT = mk_sbuf(stP, "cosT", [128, T], BF16)
                sinT = mk_sbuf(stP, "sinT", [128, T], BF16)
                p.dma(cosT[:, :], I["ropec"], writes=["cosT"], eng="POOL")
                p.dma(sinT[:, :], I["ropes"], writes=["sinT"], eng="POOL")
                rT_sb = mk_sbuf(stP, "rT_sb", [128, 64], BF16)
                p.dma(rT_sb[:, :], I["aRT"], writes=["rT_sb"], eng="POOL")
                wbuf = [mk_sbuf(stP, f"wbuf{i}", [128, 16 * 128], BF16) for i in range(2)]
                kraw = mk_sbuf(stP, "kraw", [128, T], F32)
                krot = mk_sbuf(stP, "krot", [128, T], F32)
                ktmp = mk_sbuf(stP, "ktmp", [128, T], F32)
                fb_sb = mk_sbuf(stP, "fb_sb", [128, T], BF16)
                rown = mk_sbuf(stP, "rown", [1, T], F32)
                pc_sb = mk_sbuf(stP, "pc_sb", [64, T], F32)
                uc_sb = mk_sbuf(stP, "uc_sb", [64, 512], F32)
                ic_sb = mk_sbuf(stP, "ic_sb", [64, 512], I32)
                wc_sb = mk_sbuf(stP, "wc_sb", [64, 512], F32)
                gc_sb = mk_sbuf(stP, "gc_sb", [64, 512], F32)

                def proj_head(wi, out_emit):
                    wb = wbuf[wi]
                    rw = f"wbuf{wi}"
                    for tc in range(4):
                        bank = PS[4 + (tc % 2)]
                        rb = f"ps{4 + (tc % 2)}"
                        fns = []
                        for k in range(ND):
                            fns.append(lambda t, k=k, tc=tc, wb=wb, bank=bank: t.matmul(
                                bank[:, :], wb[:, k * 128:(k + 1) * 128],
                                yTt(k)[:, tc * 512:(tc + 1) * 512],
                                start=(k == 0), stop=(k == ND - 1)))
                        p.group("PE", fns, reads=[rw] + [f"yT{k}" for k in range(ND)],
                                writes=[rb])
                        out_emit(tc, bank, rb)

                def rope_feat():
                    p.op("DVE", lambda v, kr=krot, ka=kraw: v.tensor_scalar_mul(
                        kr[0:64, :], ka[64:128, :], -1.0), reads=["kraw"], writes=["krot"])
                    p.op("DVE", lambda v, kr=krot, ka=kraw: v.tensor_copy(
                        kr[64:128, :], ka[0:64, :]), reads=["kraw"], writes=["krot"])
                    p.op("DVE", lambda v, kt=ktmp, ka=kraw, ct=cosT: v.tensor_mul(
                        kt[:, :], ka[:, :], ct[:, :]), reads=["kraw", "cosT"], writes=["ktmp"])
                    p.op("DVE", lambda v, kr=krot, st_=sinT: v.tensor_mul(
                        kr[:, :], kr[:, :], st_[:, :]), reads=["krot", "sinT"], writes=["krot"])
                    p.op("DVE", lambda v, ka=kraw, kt=ktmp, kr=krot: v.tensor_add(
                        ka[:, :], kt[:, :], kr[:, :]), reads=["ktmp", "krot"], writes=["kraw"])
                    p.op("DVE", lambda v, fb=fb_sb, ka=kraw: v.scalar_tensor_tensor(
                        fb[:, :], ka[:, :], 1.0, ka[:, :], ALU.mult, ALU.mult),
                        reads=["kraw"], writes=["fb"])
                    for tc in range(4):
                        p.op("PE", lambda t, tc=tc, fb=fb_sb: t.matmul(
                            PS[4][0:1, :], onescolB[:, :], fb[:, tc * 512:(tc + 1) * 512],
                            start=True, stop=True), reads=["fb", "onescolB"], writes=["ps4"])
                        p.op("DVE", lambda v, tc=tc, rn=rown: v.tensor_scalar_max(
                            rn[:, tc * 512:(tc + 1) * 512], PS[4][0:1, :], 1e-12),
                            reads=["ps4"], writes=["rown"])
                    p.op("ACT", lambda s, rn=rown: s.activation(rn[:, :], rn[:, :], AF.Sqrt),
                         reads=["rown"], writes=["rown"])
                    p.op("DVE", lambda v, rn=rown: v.reciprocal(rn[:, :], rn[:, :]),
                         reads=["rown"], writes=["rown"])
                    for tc in range(4):
                        p.op("PE", lambda t, tc=tc, rn=rown: t.matmul(
                            PS[4][:, :], ones1F[:, :], rn[:, tc * 512:(tc + 1) * 512],
                            start=True, stop=True), reads=["rown", "ones1F"], writes=["ps4"])
                        p.op("DVE", lambda v, tc=tc, fb=fb_sb, ka=kraw: v.tensor_mul(
                            fb[:, tc * 512:(tc + 1) * 512],
                            ka[:, tc * 512:(tc + 1) * 512], PS[4][:, :]),
                            reads=["ps4", "kraw", "fb"], writes=["fb"])
                    for tc in range(4):
                        p.op("PE", lambda t, tc=tc, rt=rT_sb, fb=fb_sb: t.matmul(
                            PS[5][0:64, :], rt[:, :], fb[:, tc * 512:(tc + 1) * 512],
                            start=True, stop=True), reads=["fb", "rT_sb"], writes=["ps5"])
                        p.op("DVE", lambda v, tc=tc, pc=pc_sb: v.tensor_copy(
                            pc[:, tc * 512:(tc + 1) * 512], PS[5][0:64, :]),
                            reads=["ps5"], writes=["pc"])
                    for tc in range(4):
                        sl = slice(tc * 512, (tc + 1) * 512)
                        for phase, shift, bias in ((0, 0.5, 0.0), (1, 0.75, np.pi / 2)):
                            p.op("DVE", lambda v, sl=sl, uc=uc_sb, pc=pc_sb, sh=shift: v.tensor_scalar(
                                uc[:, :], pc[:, sl], INV2PI, sh, ALU.mult, ALU.add),
                                reads=["pc"], writes=["uc"])
                            p.op("DVE", lambda v, ic=ic_sb, uc=uc_sb: v.tensor_copy(
                                ic[:, :], uc[:, :]), reads=["uc"], writes=["ic"])
                            p.op("DVE", lambda v, wc=wc_sb, ic=ic_sb: v.tensor_copy(
                                wc[:, :], ic[:, :]), reads=["ic"], writes=["wc"])
                            p.op("DVE", lambda v, gc=gc_sb, wc=wc_sb, uc=uc_sb: v.tensor_tensor(
                                gc[:, :], wc[:, :], uc[:, :], ALU.is_gt),
                                reads=["wc", "uc"], writes=["gc"])
                            p.op("DVE", lambda v, wc=wc_sb, gc=gc_sb: v.tensor_sub(
                                wc[:, :], wc[:, :], gc[:, :]), reads=["wc", "gc"], writes=["wc"])
                            if bias == 0.0:
                                p.op("DVE", lambda v, sl=sl, wc=wc_sb, pc=pc_sb: v.scalar_tensor_tensor(
                                    wc[:, :], wc[:, :], -TWOPI, pc[:, sl], ALU.mult, ALU.add),
                                    reads=["wc", "pc"], writes=["wc"])
                            else:
                                p.op("DVE", lambda v, sl=sl, uc=uc_sb, pc=pc_sb, bi=bias: v.tensor_scalar_add(
                                    uc[:, :], pc[:, sl], bi), reads=["pc"], writes=["uc"])
                                p.op("DVE", lambda v, wc=wc_sb, uc=uc_sb: v.scalar_tensor_tensor(
                                    wc[:, :], wc[:, :], -TWOPI, uc[:, :], ALU.mult, ALU.add),
                                    reads=["wc", "uc"], writes=["wc"])
                            p.op("DVE", lambda v, wc=wc_sb: v.tensor_scalar(
                                wc[:, :], wc[:, :], -np.pi, np.pi, ALU.max, ALU.min),
                                reads=["wc"], writes=["wc"])
                            r0 = 0 if phase == 0 else 64
                            p.op("ACT", lambda s, sl=sl, fb=fb_sb, wc=wc_sb, r0=r0: s.activation(
                                fb[r0:r0 + 64, sl], wc[:, :], AF.Sin),
                                reads=["wc", "fb"], writes=["fb"])

                for g in range(HKV):
                    wi = g % 2
                    p.dma(wbuf[wi][:, :],
                          I["aWkT"].rearrange("(k p) o -> p k o", p=128)[:, :, g * 128:(g + 1) * 128],
                          writes=[f"wbuf{wi}"], eng="POOL")

                    def k_evac(tc, bank, rb, g=g):
                        p.op("ACT", lambda s, tc=tc, bank=bank, g=g, ka=kraw: s.activation(
                            ka[:, tc * 512:(tc + 1) * 512], bank[:, :], AF.Identity,
                            bias=bk_sb[:, g:g + 1]), reads=[rb, "bk_sb"], writes=["kraw"])
                    proj_head(wi, k_evac)
                    rope_feat()
                    p.op("DVE", lambda v, g=g, kp=kp_sb, fb=fb_sb: v.tensor_copy(
                        kp[:, g * T:(g + 1) * T], fb[:, :]), reads=["fb"], writes=[f"kp{g}"])
                    p.dma(wbuf[wi][:, :],
                          I["aWvT"].rearrange("(k p) o -> p k o", p=128)[:, :, g * 128:(g + 1) * 128],
                          writes=[f"wbuf{wi}"], eng="POOL")

                    def v_evac(tc, bank, rb, g=g):
                        p.op("ACT", lambda s, tc=tc, bank=bank, g=g, vs=v_sb: s.activation(
                            vs[:, g * T + tc * 512:g * T + (tc + 1) * 512], bank[:, :],
                            AF.Identity, bias=bv_sb[:, g:g + 1]),
                            reads=[rb, "bv_sb"], writes=[f"v{g}"])
                    proj_head(wi, v_evac)

                for h in range(HA):
                    wi = h % 2
                    p.dma(wbuf[wi][:, :],
                          I["aWqT"].rearrange("(k p) o -> p k o", p=128)[:, :, h * 128:(h + 1) * 128],
                          writes=[f"wbuf{wi}"], eng="POOL")

                    def q_evac(tc, bank, rb, h=h):
                        p.op("ACT", lambda s, tc=tc, bank=bank, h=h, ka=kraw: s.activation(
                            ka[:, tc * 512:(tc + 1) * 512], bank[:, :], AF.Identity,
                            bias=bq_sb[:, h:h + 1]), reads=[rb, "bq_sb"], writes=["kraw"])
                    proj_head(wi, q_evac)
                    rope_feat()
                    p.dma(q_shard[h * 128:(h + 1) * 128, :], fb_sb[:, :],
                          reads=["fb"], writes=[f"qp{h}"])
                p.barrier()

            with ExitStack() as stM:
                qp_t = mk_sbuf(stM, "qp_t", [128, T], BF16)
                prod = mk_sbuf(stM, "prod", [128, T], BF16)
                term = [mk_sbuf(stM, f"term{i}", [128, 512], BF16) for i in range(2)]
                o_col = mk_sbuf(stM, "o_col", [128, T], BF16)
                o_row = mk_sbuf(stM, "o_row", [128, T], BF16)
                for h in range(HA):
                    p.dma(qp_t[:, :], q_shard[h * 128:(h + 1) * 128, :],
                          reads=[f"qp{h}"], writes=["qp_t"])
                    for g in range(HKV):
                        p.op("DVE", lambda v, g=g, pr=prod, qt=qp_t, kp=kp_sb: v.tensor_mul(
                            pr[:, :], qt[:, :], kp[:, g * T:(g + 1) * T]),
                            reads=["qp_t", f"kp{g}"], writes=["prod"])
                        for tc in range(4):
                            ba = PS[4 + (tc % 2)]
                            rba = f"ps{4 + (tc % 2)}"
                            p.op("PE", lambda t, tc=tc, ba=ba, pr=prod: t.matmul(
                                ba[:, :], ones2F[:, :], pr[:, tc * 512:(tc + 1) * 512],
                                start=True, stop=True), reads=["prod", "ones2F"], writes=[rba])
                            ti = tc % 2
                            p.op("DVE", lambda v, tc=tc, g=g, ba=ba, tm=term[ti], vs=v_sb: v.tensor_mul(
                                tm[:, :], ba[:, :],
                                vs[:, g * T + tc * 512:g * T + (tc + 1) * 512]),
                                reads=[rba, f"v{g}"], writes=[f"term{ti}"])
                            p.op("PE", lambda t, tc=tc, g=g, tm=term[ti]: t.matmul(
                                PS[tc][:, :], identB[:, :], tm[:, :],
                                start=(g == 0), stop=(g == HKV - 1)),
                                reads=[f"term{ti}", "identB"], writes=[f"ps{tc}"])
                    for tc in range(4):
                        p.op("ACT", lambda s, tc=tc, oc=o_col: s.copy(
                            oc[:, tc * 512:(tc + 1) * 512], PS[tc][:, :]),
                            reads=[f"ps{tc}"], writes=["o_col"])
                    for lt in range(NT):
                        bank = PB[lt % 2]
                        p.op("PE", lambda t, lt=lt, bank=bank, oc=o_col: t.transpose(
                            bank[:, 0:128], oc[:, lt * 128:(lt + 1) * 128], identB[:, :]),
                            reads=["o_col", "identB"], writes=[f"pb{lt % 2}"])
                        p.op("DVE", lambda v, lt=lt, bank=bank, orw=o_row: v.tensor_copy(
                            orw[:, lt * 128:(lt + 1) * 128], bank[:, 0:128]),
                            reads=[f"pb{lt % 2}"], writes=["o_row"])
                    dst = bass.AP(aag_in.tensor, h * 128 * D,
                                  [[D, 8], [128, 16], [8 * D, 16], [1, 128]])
                    p.dma(dst, o_row[:, :], reads=["o_row"], writes=[f"oag{h}"])
                p.barrier()

        p.collective("AllGather", ALU.bypass, [aag_in], [aag_out],
                     reads=[f"oag{h}" for h in range(HA)], writes=["aag_out"])

        with ExitStack() as stG:
            scr = mk_sbuf(stG, "scr", [128, D], BF16)
            for ltp in range(NT):
                b, k = ltp // 8, ltp % 8
                const = (2 * k * 128 + 8 * b * 8) * D
                def src_ap(e, const=const):
                    off = p.emit_off(e, "SP", const, cq=128 * D, cm=16 * D)
                    return bass.AP(aag_out.tensor, off,
                                   [[T * D, 8], [D, 16], [1, D]])
                p.dma(scr[:, :], src_ap, reads=["aag_out"], writes=["scr"])
                for o in range(ND):
                    bank = PB[o % 2]
                    p.op("PE", lambda t, o=o, bank=bank, sc=scr: t.transpose(
                        bank[:, 0:128], sc[:, o * 128:(o + 1) * 128], identB[:, :]),
                        reads=["scr", "identB"], writes=[f"pb{o % 2}"])
                    p.op("DVE", lambda v, o=o, ltp=ltp, bank=bank: v.tensor_copy(
                        bass.AP(yT, o * T + ltp * 128,
                                [[ND * T, 128], [8, 8], [64, 2], [1, 8]]),
                        bass.AP(bank, 0, [[1024, 128], [16, 8], [8, 2], [1, 8]])),
                        reads=[f"pb{o % 2}"], writes=[f"yT{o}"])
            p.barrier()

        stWB = stack.enter_context(ExitStack())
        cbuf = mk_sbuf(stWB, "cbuf", [128, ND * T], BF16)
        with ExitStack() as stW:
            wbuf2 = [mk_sbuf(stW, f"wobuf{i}", [128, 16 * 128], BF16) for i in range(2)]
            for o in range(ND):
                wi = o % 2
                p.dma(wbuf2[wi][:, :],
                      I["aWoT"].rearrange("(k p) o -> p k o", p=128)[:, :, o * 128:(o + 1) * 128],
                      writes=[f"wobuf{wi}"], eng="POOL")
                for tc in range(4):
                    bank = PS[4 + (tc % 2)]
                    rb = f"ps{4 + (tc % 2)}"
                    fns = []
                    for k in range(ND):
                        fns.append(lambda t, k=k, tc=tc, wb=wbuf2[wi], bank=bank: t.matmul(
                            bank[:, :], wb[:, k * 128:(k + 1) * 128],
                            yTt(k)[:, tc * 512:(tc + 1) * 512],
                            start=(k == 0), stop=(k == ND - 1)))
                    p.group("PE", fns, reads=[f"wobuf{wi}"] + [f"yT{k}" for k in range(ND)],
                            writes=[rb])
                    p.op("ACT", lambda s, tc=tc, o=o, bank=bank, cb=cbuf: s.activation(
                        cb[:, o * T + tc * 512:o * T + (tc + 1) * 512], bank[:, :],
                        AF.Identity, bias=bo_sb[:, o:o + 1]),
                        reads=[rb, "bo_sb"], writes=[f"cb{o}"])
            p.barrier()

        with ExitStack() as st1:
            emit_boundary("b1", I["x"], cbuf, "cb", xr1, "xr1", st1, do_ln=True)
            p.barrier()
        stWB.close()

        def emit_ffn(tag, src_rows, src_region, dst_rows, dst_region, st,
                     do_ln=True, final_out=False):
            combT = mk_sbuf(st, f"{tag}_combT", [32, T], BF16)
            with ExitStack() as sr:
                lg = mk_sbuf(sr, f"{tag}_lg", [128, NE], F32)
                pr = mk_sbuf(sr, f"{tag}_pr", [128, NE], F32)
                tw = mk_sbuf(sr, f"{tag}_tw", [128, NE], F32)
                pm = mk_sbuf(sr, f"{tag}_pm", [128, NE], F32)
                act = mk_sbuf(sr, f"{tag}_act", [128, NE], F32)
                mc = mk_sbuf(sr, f"{tag}_mc", [128, 16], F32)
                for lt in range(NT):
                    fns = []
                    for k in range(ND):
                        fns.append(lambda t, k=k, lt=lt: t.matmul(
                            PS[3][:, 0:NE], yTt(k)[:, lt * 128:(lt + 1) * 128],
                            mWgT_sb[:, k * NE:(k + 1) * NE],
                            start=(k == 0), stop=(k == ND - 1)))
                    p.group("PE", fns, reads=["mWgT"] + [f"yT{k}" for k in range(ND)],
                            writes=["ps3"])
                    R = f"{tag}_r"
                    p.op("DVE", lambda v, lg=lg: v.tensor_copy(lg[:, :], PS[3][:, 0:NE]),
                         reads=["ps3"], writes=[R])
                    p.op("DVE", lambda v, lg=lg, mc=mc: v.reduce_max(
                        mc[:, 0:1], lg[:, :], axis=AX.X), reads=[R], writes=[R])
                    p.op("DVE", lambda v, mc=mc: v.tensor_scalar_mul(
                        mc[:, 1:2], mc[:, 0:1], -1.0), reads=[R], writes=[R])
                    p.op("ACT", lambda s, lg=lg, pr=pr, mc=mc: s.activation(
                        pr[:, :], lg[:, :], AF.Exp, bias=mc[:, 1:2],
                        accum_out=mc[:, 2:3]), reads=[R], writes=[R])
                    p.op("DVE", lambda v, mc=mc: v.reciprocal(mc[:, 3:4], mc[:, 2:3]),
                         reads=[R], writes=[R])
                    p.op("DVE", lambda v, pr=pr, mc=mc: v.tensor_scalar_mul(
                        pr[:, :], pr[:, :], mc[:, 3:4]), reads=[R], writes=[R])
                    p.op("DVE", lambda v, tw=tw, pr=pr: v.tensor_copy(tw[:, :], pr[:, :]),
                         reads=[R], writes=[R])
                    for i8 in range(TOPK):
                        p.op("DVE", lambda v, tw=tw, mc=mc, i8=i8: v.reduce_max(
                            mc[:, 8 + i8:9 + i8], tw[:, :], axis=AX.X),
                            reads=[R], writes=[R])
                        if i8 < TOPK - 1:
                            p.op("DVE", lambda v, pm=pm, tw=tw, mc=mc, i8=i8: v.tensor_scalar(
                                pm[:, :], tw[:, :], mc[:, 8 + i8:9 + i8], None, ALU.is_ge),
                                reads=[R], writes=[R])
                            p.op("DVE", lambda v, tw=tw, pm=pm: v.scalar_tensor_tensor(
                                tw[:, :], pm[:, :], -2.0, tw[:, :], ALU.mult, ALU.add),
                                reads=[R], writes=[R])
                    p.op("DVE", lambda v, pm=pm, pr=pr, mc=mc: v.tensor_scalar(
                        pm[:, :], pr[:, :], mc[:, 15:16], None, ALU.is_ge),
                        reads=[R], writes=[R])
                    p.op("DVE", lambda v, act=act, pr=pr, pm=pm, mc=mc: v.scalar_tensor_tensor(
                        act[:, :], pr[:, :], 1.0, pm[:, :], ALU.mult, ALU.mult,
                        accum_out=mc[:, 4:5]), reads=[R], writes=[R])
                    p.op("DVE", lambda v, mc=mc: v.tensor_scalar_mul(
                        mc[:, 5:6], mc[:, 4:5], TAU), reads=[R, "alpha_col"], writes=[R])
                    p.op("DVE", lambda v, act=act, pr=pr, mc=mc: v.tensor_scalar(
                        act[:, :], pr[:, :], alpha_col[:, 0:1], mc[:, 5:6],
                        ALU.mult, ALU.subtract), reads=[R, "alpha_col"], writes=[R])
                    p.op("DVE", lambda v, act=act: v.tensor_single_scalar(
                        act[:, :], act[:, :], 0.0, ALU.is_ge), reads=[R], writes=[R])
                    p.op("DVE", lambda v, act=act, pm=pm: v.tensor_mul(
                        act[:, :], act[:, :], pm[:, :]), reads=[R], writes=[R])
                    p.op("DVE", lambda v, act=act, mc=mc: v.reduce_sum(
                        mc[:, 6:7], act[:, :], axis=AX.X), reads=[R], writes=[R])
                    p.op("DVE", lambda v, mc=mc: v.tensor_single_scalar(
                        mc[:, 6:7], mc[:, 6:7], 0.5, ALU.is_lt), reads=[R], writes=[R])
                    p.op("DVE", lambda v, mc=mc: v.tensor_single_scalar(
                        mc[:, 7:8], alpha_col[:, 0:1], 0.0, ALU.is_ge),
                        reads=["alpha_col"], writes=[R])
                    p.op("DVE", lambda v, mc=mc: v.tensor_mul(
                        mc[:, 1:2], mc[:, 7:8], mc[:, 8:9]), reads=[R], writes=[R])
                    p.op("DVE", lambda v, mc=mc: v.tensor_scalar(
                        mc[:, 2:3], mc[:, 7:8], -1.0, 1.0, ALU.mult, ALU.add),
                        reads=[R], writes=[R])
                    p.op("DVE", lambda v, mc=mc: v.tensor_mul(
                        mc[:, 2:3], mc[:, 2:3], mc[:, 15:16]), reads=[R], writes=[R])
                    p.op("DVE", lambda v, mc=mc: v.tensor_add(
                        mc[:, 1:2], mc[:, 1:2], mc[:, 2:3]), reads=[R], writes=[R])
                    p.op("DVE", lambda v, tw=tw, pr=pr, mc=mc: v.tensor_scalar(
                        tw[:, :], pr[:, :], mc[:, 1:2], None, ALU.is_equal),
                        reads=[R], writes=[R])
                    p.op("DVE", lambda v, tw=tw, pm=pm: v.tensor_mul(
                        tw[:, :], tw[:, :], pm[:, :]), reads=[R], writes=[R])
                    p.op("DVE", lambda v, tw=tw, mc=mc: v.tensor_scalar_mul(
                        tw[:, :], tw[:, :], mc[:, 6:7]), reads=[R], writes=[R])
                    p.op("DVE", lambda v, act=act, tw=tw: v.tensor_add(
                        act[:, :], act[:, :], tw[:, :]), reads=[R], writes=[R])
                    p.op("DVE", lambda v, pm=pm, pr=pr, act=act, mc=mc: v.scalar_tensor_tensor(
                        pm[:, :], pr[:, :], 1.0, act[:, :], ALU.mult, ALU.mult,
                        accum_out=mc[:, 4:5]), reads=[R], writes=[R])
                    p.op("DVE", lambda v, mc=mc: v.tensor_single_scalar(
                        mc[:, 4:5], mc[:, 4:5], 1e-12, ALU.max), reads=[R], writes=[R])
                    p.op("DVE", lambda v, mc=mc: v.reciprocal(
                        mc[:, 5:6], mc[:, 4:5]), reads=[R], writes=[R])
                    p.op("DVE", lambda v, pm=pm, mc=mc: v.tensor_scalar_mul(
                        pm[:, :], pm[:, :], mc[:, 5:6]), reads=[R], writes=[R])
                    p.op("PE", lambda t, pm=pm: t.transpose(
                        PS[2][0:NE, 0:128], pm[:, :], identF[:, :]),
                        reads=[R, "identF"], writes=["ps2"])
                    p.op("DVE", lambda v, lt=lt, cT=combT: v.tensor_copy(
                        cT[:, lt * 128:(lt + 1) * 128], PS[2][0:NE, 0:128]),
                        reads=["ps2"], writes=[f"{tag}_combT"])

            acc = mk_sbuf(st, f"{tag}_acc", [128, ND * 1024], F32)
            for hf in range(2):
              with ExitStack() as sx:
                gsil = mk_sbuf(sx, f"{tag}_gsil{hf}", [128, 1024], F32)
                ht = mk_sbuf(sx, f"{tag}_ht{hf}", [128, 512], F32)
                hp = [mk_sbuf(sx, f"{tag}_hp{hf}{i}", [128, 6 * 1024], BF16) for i in range(2)]
                bcc = [mk_sbuf(sx, f"{tag}_bcc{hf}{i}", [128, 1024], BF16) for i in range(2)]
                wgu = [mk_sbuf(sx, f"{tag}_wgu{hf}{i}", [128, 16 * 128], BF16) for i in range(2)]
                wfill = (tag == "f1" and hf == 0)
                wdb = [mk_sbuf(sx, f"{tag}_wdb{hf}{i}", [128, 6 * 512], BF16) for i in range(2)]
                t0 = hf * 1024
                p.op("DVE", lambda v, acc=acc: v.memset(acc[:, :], 0.0),
                     writes=[f"{tag}_acc"])
                for grp in range(NE // 2):
                    for ei in range(2):
                        e = grp * 2 + ei
                        for cc2 in range(2):
                            p.op("PE", lambda t, e=e, cc2=cc2, t0=t0, cT=combT: t.matmul(
                                PS[2][:, :], oneh_sb[:, e * 128:(e + 1) * 128],
                                cT[:, t0 + cc2 * 512:t0 + (cc2 + 1) * 512],
                                start=True, stop=True),
                                reads=["oneh", f"{tag}_combT"], writes=["ps2"])
                            p.op("DVE", lambda v, ei=ei, cc2=cc2, bc=bcc[ei]: v.tensor_copy(
                                bc[:, cc2 * 512:(cc2 + 1) * 512], PS[2][:, :]),
                                reads=["ps2"], writes=[f"{tag}_bcc{ei}"])
                        for jj in range(6):
                            gslot = e * 12 + jj
                            if wfill:
                                p.dma(wgu[ei][:, :],
                                      I["mWguT"][e].rearrange("(k p) j -> p k j", p=128)[:, :, jj * 128:(jj + 1) * 128],
                                      writes=[f"{tag}_wgu{ei}"], eng="POOL")
                                p.dma(wguC[gslot * 128:(gslot + 1) * 128, :], wgu[ei][:, :],
                                      reads=[f"{tag}_wgu{ei}"], writes=[f"wguC{gslot}"])
                            else:
                                p.dma(wgu[ei][:, :], wguC[gslot * 128:(gslot + 1) * 128, :],
                                      reads=[f"wguC{gslot}"], writes=[f"{tag}_wgu{ei}"])
                            for tc in range(2):
                                bank = PS[4 + (tc % 2)]
                                rb = f"ps{4 + (tc % 2)}"
                                fns = []
                                for k in range(ND):
                                    fns.append(lambda t, k=k, tc=tc, t0=t0, wb=wgu[ei], bank=bank: t.matmul(
                                        bank[:, :], wb[:, k * 128:(k + 1) * 128],
                                        yTt(k)[:, t0 + tc * 512:t0 + (tc + 1) * 512],
                                        start=(k == 0), stop=(k == ND - 1)))
                                p.group("PE", fns, reads=[f"{tag}_wgu{ei}"] + [f"yT{k}" for k in range(ND)],
                                        writes=[rb])
                                p.op("ACT", lambda s, tc=tc, bank=bank, gs=gsil: s.activation(
                                    gs[:, tc * 512:(tc + 1) * 512], bank[:, :], AF.Silu),
                                    reads=[rb], writes=[f"{tag}_gsil"])
                            uslot = e * 12 + 6 + jj
                            if wfill:
                                p.dma(wgu[ei][:, :],
                                      I["mWguT"][e].rearrange("(k p) j -> p k j", p=128)[:, :, 768 + jj * 128:768 + (jj + 1) * 128],
                                      writes=[f"{tag}_wgu{ei}"], eng="POOL")
                                p.dma(wguC[uslot * 128:(uslot + 1) * 128, :], wgu[ei][:, :],
                                      reads=[f"{tag}_wgu{ei}"], writes=[f"wguC{uslot}"])
                            else:
                                p.dma(wgu[ei][:, :], wguC[uslot * 128:(uslot + 1) * 128, :],
                                      reads=[f"wguC{uslot}"], writes=[f"{tag}_wgu{ei}"])
                            for tc in range(2):
                                bank = PS[4 + (tc % 2)]
                                rb = f"ps{4 + (tc % 2)}"
                                fns = []
                                for k in range(ND):
                                    fns.append(lambda t, k=k, tc=tc, t0=t0, wb=wgu[ei], bank=bank: t.matmul(
                                        bank[:, :], wb[:, k * 128:(k + 1) * 128],
                                        yTt(k)[:, t0 + tc * 512:t0 + (tc + 1) * 512],
                                        start=(k == 0), stop=(k == ND - 1)))
                                p.group("PE", fns, reads=[f"{tag}_wgu{ei}"] + [f"yT{k}" for k in range(ND)],
                                        writes=[rb])
                                p.op("DVE", lambda v, tc=tc, bank=bank, gs=gsil, h1=ht: v.tensor_mul(
                                    h1[:, :], gs[:, tc * 512:(tc + 1) * 512], bank[:, :]),
                                    reads=[rb, f"{tag}_gsil"], writes=[f"{tag}_ht"])
                                p.op("DVE", lambda v, tc=tc, jj=jj, h1=ht, hpp=hp[ei], bc=bcc[ei]: v.tensor_mul(
                                    hpp[:, jj * 1024 + tc * 512:jj * 1024 + (tc + 1) * 512],
                                    h1[:, :], bc[:, tc * 512:(tc + 1) * 512]),
                                    reads=[f"{tag}_ht", f"{tag}_bcc{ei}"], writes=[f"{tag}_hp{ei}"])
                    for dmc in range(4):
                        for ei in range(2):
                            e = grp * 2 + ei
                            dslot = e * 4 + dmc
                            if wfill:
                                p.dma(wdb[ei][:, :],
                                      I["mWdT"][e].rearrange("(k p) m -> p k m", p=128)[:, :, dmc * 512:(dmc + 1) * 512],
                                      writes=[f"{tag}_wdb{ei}"], eng="POOL")
                                p.dma(wdC[dslot * 128:(dslot + 1) * 128, :], wdb[ei][:, :],
                                      reads=[f"{tag}_wdb{ei}"], writes=[f"wdC{dslot}"])
                            else:
                                p.dma(wdb[ei][:, :], wdC[dslot * 128:(dslot + 1) * 128, :],
                                      reads=[f"wdC{dslot}"], writes=[f"{tag}_wdb{ei}"], eng="POOL")
                        for dms in range(4):
                            dm = dmc * 4 + dms
                            for tc in range(2):
                                fns = []
                                for ei in range(2):
                                    for k in range(6):
                                        fns.append(lambda t, ei=ei, k=k, tc=tc, dms=dms, wd=wdb[ei], hpp=hp[ei]: t.matmul(
                                            PS[tc][:, :], wd[:, k * 512 + dms * 128:k * 512 + (dms + 1) * 128],
                                            hpp[:, k * 1024 + tc * 512:k * 1024 + (tc + 1) * 512],
                                            start=(ei == 0 and k == 0), stop=(ei == 1 and k == 5)))
                                p.group("PE", fns,
                                        reads=[f"{tag}_wdb0", f"{tag}_wdb1", f"{tag}_hp0", f"{tag}_hp1"],
                                        writes=[f"ps{tc}"])
                                p.op("DVE", lambda v, dm=dm, tc=tc, acc=acc: v.tensor_tensor(
                                    acc[:, dm * 1024 + tc * 512:dm * 1024 + (tc + 1) * 512],
                                    acc[:, dm * 1024 + tc * 512:dm * 1024 + (tc + 1) * 512],
                                    PS[tc][:, :], ALU.add),
                                    reads=[f"ps{tc}", f"{tag}_acc"], writes=[f"{tag}_acc"])
                p.barrier()
              for o in range(ND):
                  p.writer[f"{tag}_accv{o}"] = p.writer.get(f"{tag}_acc")
              with ExitStack() as sb2:
                  emit_boundary(f"{tag}h{hf}", src_rows, acc, f"{tag}_accv",
                                dst_rows, dst_region, sb2,
                                do_ln=do_ln, lts=range(hf * 8, (hf + 1) * 8),
                                cb_w=1024, cb_lt0=hf * 8, cb_f32=True,
                                src_region=src_region)
              p.barrier()

        with ExitStack() as st2:
            emit_ffn("f1", xr1, "xr1", xr2, "xr2", st2, do_ln=True)


        with ExitStack() as s3:
            colkv = mk_sbuf(s3, "colkv", [128, 4], F32)
            colq = mk_sbuf(s3, "colq", [128, 12], F32)
            p.dma(colkv[:, :], I["bkvnw"].rearrange("(o p) -> p o", p=128),
                  writes=["colkv"], allow_slow_non_contiguous=True)
            p.dma(colq[:, :], I["bqnw"].rearrange("(o p) -> p o", p=128),
                  writes=["colq"], allow_slow_non_contiguous=True)
            fcs = mk_sbuf(s3, "fcs", [16, 32], F32)
            fss = mk_sbuf(s3, "fss", [16, 32], F32)
            p.dma(fcs[:, :], I["bfc"], writes=["fcs"])
            p.dma(fss[:, :], I["bfs"], writes=["fss"])
            fcT = mk_sbuf(s3, "fcT", [32, 16], F32)
            fsT = mk_sbuf(s3, "fsT", [32, 16], F32)
            nfsT = mk_sbuf(s3, "nfsT", [32, 16], F32)
            p.op("PE", lambda t: t.transpose(PS[2][0:32, 0:16], fcs[:, :], identF[0:16, 0:16]),
                 reads=["fcs", "identF"], writes=["ps2"])
            p.op("DVE", lambda v: v.tensor_copy(fcT[:, :], PS[2][0:32, 0:16]),
                 reads=["ps2"], writes=["fcT"])
            p.op("PE", lambda t: t.transpose(PS[2][0:32, 0:16], fss[:, :], identF[0:16, 0:16]),
                 reads=["fss", "identF"], writes=["ps2"])
            p.op("DVE", lambda v: v.tensor_copy(fsT[:, :], PS[2][0:32, 0:16]),
                 reads=["ps2"], writes=["fsT"])
            p.op("DVE", lambda v: v.tensor_scalar_mul(nfsT[:, :], fsT[:, :], -1.0),
                 reads=["fsT"], writes=["nfsT"])

            wch = [mk_sbuf(s3, f"b_wch{i}", [128, 16 * 128], BF16) for i in range(2)]
            stg = mk_sbuf(s3, "b_stg", [128, T], BF16)
            sqb = mk_sbuf(s3, "b_sqb", [128, T], BF16)
            rowA = mk_sbuf(s3, "b_rowA", [1, T], F32)
            rstd_bc = mk_sbuf(s3, "b_rstdbc", [128, T], F32)
            nkvn = mk_sbuf(s3, "b_nkvn", [128, 4 * T], BF16)

            def rows_finalize(nfeat):
                p.op("DVE", lambda v, rw=rowA, nf=nfeat: v.tensor_scalar(
                    rw[:, :], rw[:, :], 1.0 / nf, EPS_LN, ALU.mult, ALU.add),
                    reads=["rowA"], writes=["rowA"])
                p.op("ACT", lambda s, rw=rowA: s.activation(rw[:, :], rw[:, :], AF.Sqrt),
                     reads=["rowA"], writes=["rowA"])
                p.op("DVE", lambda v, rw=rowA: v.reciprocal(rw[:, :], rw[:, :]),
                     reads=["rowA"], writes=["rowA"])
                for tc in range(4):
                    p.op("PE", lambda t, tc=tc, rw=rowA: t.matmul(
                        PS[2][:, :], ones1F[:, :], rw[:, tc * 512:(tc + 1) * 512],
                        start=True, stop=True), reads=["rowA", "ones1F"], writes=["ps2"])
                    p.op("DVE", lambda v, tc=tc, rb=rstd_bc: v.tensor_copy(
                        rb[:, tc * 512:(tc + 1) * 512], PS[2][:, :]),
                        reads=["ps2"], writes=["rstd_bc"])

            def bigproj(nsrc, src_sb, src_region, nK, wname, wI, colw, ncols_w,
                        out_emit, oo, scale_col=True):
                wb = wch[oo % 2]
                rw = f"b_wch{oo % 2}"
                p.dma(wb[:, 0:nK * 128],
                      wI.rearrange("(k p) o -> p k o", p=128)[:, :, oo * 128:(oo + 1) * 128],
                      writes=[rw], eng="POOL")
                if scale_col:
                    for k in range(nK):
                        p.op("DVE", lambda v, k=k, wb=wb, cw=colw: v.tensor_scalar_mul(
                            wb[:, k * 128:(k + 1) * 128], wb[:, k * 128:(k + 1) * 128],
                            cw[:, k:k + 1]), reads=[rw, ncols_w], writes=[rw])
                for tc in range(4):
                    bank = PS[4 + (tc % 2)]
                    rb = f"ps{4 + (tc % 2)}"
                    fns = []
                    for k in range(nK):
                        fns.append(lambda t, k=k, tc=tc, wb=wb, bank=bank, ss=src_sb: t.matmul(
                            bank[:, :], wb[:, k * 128:(k + 1) * 128],
                            (yTt(k) if ss is None else ss[:, k * T:(k + 1) * T])[:, tc * 512:(tc + 1) * 512],
                            start=(k == 0), stop=(k == nK - 1)))
                    p.group("PE", fns,
                            reads=[rw] + ([f"yT{k}" for k in range(ND)] if src_sb is None else [src_region]),
                            writes=[rb])
                    out_emit(tc, bank, rb)

            with ExitStack() as s3a:
                nkv_f = mk_sbuf(s3a, "b_nkvf", [128, 4 * T], F32)
                for o in range(4):
                    def nkv_evac(tc, bank, rb, o=o):
                        p.op("DVE", lambda v, o=o, tc=tc, bank=bank, nf=nkv_f: v.tensor_copy(
                            nf[:, o * T + tc * 512:o * T + (tc + 1) * 512], bank[:, :]),
                            reads=[rb], writes=["b_nkvf"])
                    bigproj(None, None, None, ND, None, I["bWckv"], None, None,
                            nkv_evac, o, scale_col=False)
                    p.op("ACT", lambda s, o=o, nf=nkv_f, sq=sqb: s.activation(
                        sq[:, :], nf[:, o * T:(o + 1) * T], AF.Square),
                        reads=["b_nkvf"], writes=["b_sqb"])
                    for tc in range(4):
                        p.op("PE", lambda t, tc=tc, sq=sqb: t.matmul(
                            PS[3][0:1, :], onescolB[:, :], sq[:, tc * 512:(tc + 1) * 512],
                            start=True, stop=True),
                            reads=["b_sqb", "onescolB"], writes=["ps3"])
                        if o == 0:
                            p.op("DVE", lambda v, tc=tc, rw=rowA: v.tensor_copy(
                                rw[:, tc * 512:(tc + 1) * 512], PS[3][0:1, :]),
                                reads=["ps3"], writes=["rowA"])
                        else:
                            p.op("DVE", lambda v, tc=tc, rw=rowA: v.tensor_tensor(
                                rw[:, tc * 512:(tc + 1) * 512], rw[:, tc * 512:(tc + 1) * 512],
                                PS[3][0:1, :], ALU.add), reads=["ps3", "rowA"], writes=["rowA"])
                rows_finalize(R_KV)
                for o in range(4):
                    p.op("DVE", lambda v, o=o, nf=nkv_f, nk=nkvn, rb=rstd_bc: v.tensor_mul(
                        nk[:, o * T:(o + 1) * T], nf[:, o * T:(o + 1) * T], rb[:, :]),
                        reads=["b_nkvf", "rstd_bc"], writes=["b_nkvn"])

            for h in range(HB):
                def kn_evac(tc, bank, rb):
                    p.op("DVE", lambda v, tc=tc, bank=bank, sg=stg: v.tensor_copy(
                        sg[:, tc * 512:(tc + 1) * 512], bank[:, :]),
                        reads=[rb], writes=["b_stg"])
                bigproj(None, nkvn, "b_nkvn", 4, None, I["bWdkn"], colkv, "colkv",
                        kn_evac, h, scale_col=True)
                p.dma(kag_in[h * 192:h * 192 + 128, :], stg[:, :],
                      reads=["b_stg"], writes=[f"kag{h}"])

            with ExitStack() as s3b:
                kr64 = mk_sbuf(s3b, "b_kr64", [64, T], F32)
                tmpi = mk_sbuf(s3b, "b_tmpi", [32, T], F32)
                rc_ = mk_sbuf(s3b, "b_rc", [32, T], F32)
                ri_ = mk_sbuf(s3b, "b_ri", [32, T], F32)
                ro_ = mk_sbuf(s3b, "b_ro", [32, T], BF16)
                p.dma(wch[0][:, 0:16 * 64],
                      I["bWkr"].rearrange("(k p) o -> p k o", p=128),
                      writes=["b_wch0"], eng="POOL")
                for tc in range(4):
                    bank = PS[4 + (tc % 2)]
                    rb = f"ps{4 + (tc % 2)}"
                    fns = []
                    for k in range(ND):
                        fns.append(lambda t, k=k, tc=tc, bank=bank: t.matmul(
                            bank[0:64, :], wch[0][:, k * 64:(k + 1) * 64],
                            yTt(k)[:, tc * 512:(tc + 1) * 512],
                            start=(k == 0), stop=(k == ND - 1)))
                    p.group("PE", fns, reads=["b_wch0"] + [f"yT{k}" for k in range(ND)],
                            writes=[rb])
                    p.op("DVE", lambda v, tc=tc, bank=bank, kr=kr64: v.tensor_copy(
                        kr[:, tc * 512:(tc + 1) * 512], bank[0:64, :]),
                        reads=[rb], writes=["b_kr64"])
                p.op("DVE", lambda v, kr=kr64, ti=tmpi: v.tensor_copy(
                    ti[:, :], kr[32:64, :]), reads=["b_kr64"], writes=["b_tmpi"])
                for h in range(HB):
                    p.op("DVE", lambda v, h=h, kr=kr64, rc=rc_: v.tensor_scalar_mul(
                        rc[:, :], kr[0:32, :], fcT[:, h:h + 1]),
                        reads=["b_kr64", "fcT"], writes=["b_rc"])
                    p.op("DVE", lambda v, h=h, kr=kr64, ri=ri_: v.tensor_scalar_mul(
                        ri[:, :], kr[0:32, :], fsT[:, h:h + 1]),
                        reads=["b_kr64", "fsT"], writes=["b_ri"])
                    p.op("DVE", lambda v, h=h, ti=tmpi, rc=rc_, ro=ro_: v.scalar_tensor_tensor(
                        ro[:, :], ti[:, :], nfsT[:, h:h + 1], rc[:, :], ALU.mult, ALU.add),
                        reads=["b_tmpi", "nfsT", "b_rc"], writes=["b_ro"])
                    p.dma(kag_in[h * 192 + 128:h * 192 + 160, :], ro_[:, :],
                          reads=["b_ro"], writes=[f"kag{h}"])
                    p.op("DVE", lambda v, h=h, ti=tmpi, ri=ri_, ro=ro_: v.scalar_tensor_tensor(
                        ro[:, :], ti[:, :], fcT[:, h:h + 1], ri[:, :], ALU.mult, ALU.add),
                        reads=["b_tmpi", "fcT", "b_ri"], writes=["b_ro"])
                    p.dma(kag_in[h * 192 + 160:h * 192 + 192, :], ro_[:, :],
                          reads=["b_ro"], writes=[f"kag{h}"])

            p.barrier()
            nqb = mk_sbuf(s3, "b_nqb", [128, 12 * T], BF16)
            with ExitStack() as s3c:
                nq_f = mk_sbuf(s3c, "b_nqf", [128, T], F32)
                for o in range(12):
                    def nq_evac(tc, bank, rb):
                        p.op("DVE", lambda v, tc=tc, bank=bank, nf=nq_f: v.tensor_copy(
                            nf[:, tc * 512:(tc + 1) * 512], bank[:, :]),
                            reads=[rb], writes=["b_nqf"])
                    bigproj(None, None, None, ND, None, I["bWcq"], None, None,
                            nq_evac, o, scale_col=False)
                    p.op("ACT", lambda s, nf=nq_f, sq=sqb: s.activation(
                        sq[:, :], nf[:, :], AF.Square), reads=["b_nqf"], writes=["b_sqb"])
                    p.op("DVE", lambda v, o=o, nf=nq_f, nq=nqb: v.tensor_copy(
                        nq[:, o * T:(o + 1) * T], nf[:, :]), reads=["b_nqf"], writes=["b_nqb"])
                    for tc in range(4):
                        p.op("PE", lambda t, tc=tc, sq=sqb: t.matmul(
                            PS[3][0:1, :], onescolB[:, :], sq[:, tc * 512:(tc + 1) * 512],
                            start=True, stop=True),
                            reads=["b_sqb", "onescolB"], writes=["ps3"])
                        if o == 0:
                            p.op("DVE", lambda v, tc=tc, rw=rowA: v.tensor_copy(
                                rw[:, tc * 512:(tc + 1) * 512], PS[3][0:1, :]),
                                reads=["ps3"], writes=["rowA"])
                        else:
                            p.op("DVE", lambda v, tc=tc, rw=rowA: v.tensor_tensor(
                                rw[:, tc * 512:(tc + 1) * 512], rw[:, tc * 512:(tc + 1) * 512],
                                PS[3][0:1, :], ALU.add), reads=["ps3", "rowA"], writes=["rowA"])
                rows_finalize(R_Q)
                for o in range(12):
                    p.op("DVE", lambda v, o=o, nq=nqb, rb=rstd_bc: v.tensor_mul(
                        nq[:, o * T:(o + 1) * T], nq[:, o * T:(o + 1) * T], rb[:, :]),
                        reads=["b_nqb", "rstd_bc"], writes=["b_nqb"])

            for h in range(HB):
                def qn_evac(tc, bank, rb):
                    p.op("DVE", lambda v, tc=tc, bank=bank, sg=stg: v.tensor_copy(
                        sg[:, tc * 512:(tc + 1) * 512], bank[:, :]),
                        reads=[rb], writes=["b_stg"])
                bigproj(None, nqb, "b_nqb", 12, None, I["bWdqn"], colq, "colq",
                        qn_evac, h, scale_col=True)
                p.dma(qag_in[h * 192:h * 192 + 128, :], stg[:, :],
                      reads=["b_stg"], writes=[f"qag{h}"])

            with ExitStack() as s3d:
                qr_f = mk_sbuf(s3d, "b_qrf", [128, T], F32)
                tmpi = mk_sbuf(s3d, "b_qtmpi", [32, T], F32)
                rc_ = mk_sbuf(s3d, "b_qrc", [32, T], F32)
                ri_ = mk_sbuf(s3d, "b_qri", [32, T], F32)
                ro_ = mk_sbuf(s3d, "b_qro", [32, T], BF16)
                for o in range(8):
                    def qr_evac(tc, bank, rb):
                        p.op("DVE", lambda v, tc=tc, bank=bank, qf=qr_f: v.tensor_copy(
                            qf[:, tc * 512:(tc + 1) * 512], bank[:, :]),
                            reads=[rb], writes=["b_qrf"])
                    bigproj(None, nqb, "b_nqb", 12, None, I["bWdqr"], colq, "colq",
                            qr_evac, o, scale_col=True)
                    for sub in range(2):
                        h = 2 * o + sub
                        rb2 = sub * 64
                        p.op("DVE", lambda v, rb2=rb2, qf=qr_f, ti=tmpi: v.tensor_copy(
                            ti[:, :], qf[rb2 + 32:rb2 + 64, :]), reads=["b_qrf"], writes=["b_qtmpi"])
                        p.op("DVE", lambda v, h=h, rb2=rb2, qf=qr_f, rc=rc_: v.tensor_scalar_mul(
                            rc[:, :], qf[rb2:rb2 + 32, :], fcT[:, h:h + 1]),
                            reads=["b_qrf", "fcT"], writes=["b_qrc"])
                        p.op("DVE", lambda v, h=h, rb2=rb2, qf=qr_f, ri=ri_: v.tensor_scalar_mul(
                            ri[:, :], qf[rb2:rb2 + 32, :], fsT[:, h:h + 1]),
                            reads=["b_qrf", "fsT"], writes=["b_qri"])
                        p.op("DVE", lambda v, h=h, ti=tmpi, rc=rc_, ro=ro_: v.scalar_tensor_tensor(
                            ro[:, :], ti[:, :], nfsT[:, h:h + 1], rc[:, :], ALU.mult, ALU.add),
                            reads=["b_qtmpi", "nfsT", "b_qrc"], writes=["b_qro"])
                        p.dma(qag_in[h * 192 + 128:h * 192 + 160, :], ro_[:, :],
                              reads=["b_qro"], writes=[f"qag{h}"])
                        p.op("DVE", lambda v, h=h, ti=tmpi, ri=ri_, ro=ro_: v.scalar_tensor_tensor(
                            ro[:, :], ti[:, :], fcT[:, h:h + 1], ri[:, :], ALU.mult, ALU.add),
                            reads=["b_qtmpi", "fcT", "b_qri"], writes=["b_qro"])
                        p.dma(qag_in[h * 192 + 160:h * 192 + 192, :], ro_[:, :],
                              reads=["b_qro"], writes=[f"qag{h}"])

            p.barrier()
            s3v = s3.enter_context(ExitStack())
            wdv_sb = mk_sbuf(s3v, "b_wdv", [128, 4 * D], BF16)
            p.dma(wdv_sb[:, :], I["bWdv"].rearrange("(k p) o -> p k o", p=128),
                  writes=["b_wdv"], eng="POOL")
            for k in range(4):
                p.op("DVE", lambda v, k=k: v.tensor_scalar_mul(
                    wdv_sb[:, k * D:(k + 1) * D], wdv_sb[:, k * D:(k + 1) * D],
                    colkv[:, k:k + 1]), reads=["b_wdv", "colkv"], writes=["b_wdv"])
            VSC = 1.0 / float(np.sqrt(HB * D_V))
            for lt in range(NT):
                for dvc in range(4):
                    bank = PS[4 + (dvc % 2)]
                    rb = f"ps{4 + (dvc % 2)}"
                    fns = []
                    for k in range(4):
                        fns.append(lambda t, k=k, dvc=dvc, lt=lt, bank=bank: t.matmul(
                            bank[:, :], nkvn[:, k * T + lt * 128:k * T + (lt + 1) * 128],
                            wdv_sb[:, k * D + dvc * 512:k * D + (dvc + 1) * 512],
                            start=(k == 0), stop=(k == 3)))
                    p.group("PE", fns, reads=["b_nkvn", "b_wdv"], writes=[rb])
                    p.op("ACT", lambda s, dvc=dvc, bank=bank, sg=stg: s.activation(
                        sg[:, dvc * 512:(dvc + 1) * 512], bank[:, :], AF.Identity,
                        scale=VSC), reads=[rb], writes=["b_stg"])
                p.dma(vag_in[lt * 128:(lt + 1) * 128, :], stg[:, :],
                      reads=["b_stg"], writes=[f"vag{lt}"])
            p.barrier()

        p.collective("AllGather", ALU.bypass, [qag_in], [qag_out],
                     reads=[f"qag{h}" for h in range(HB)], writes=["qag_out"])
        p.collective("AllGather", ALU.bypass, [kag_in], [kag_out],
                     reads=[f"kag{h}" for h in range(HB)], writes=["kag_out"])
        p.collective("AllGather", ALU.bypass, [vag_in], [vag_out],
                     reads=[f"vag{lt}" for lt in range(NT)], writes=["vag_out"])

        with ExitStack() as s4:
            kA = mk_sbuf(s4, "at_kA", [128, S], BF16)
            kB = mk_sbuf(s4, "at_kB", [64, S], BF16)
            qA = mk_sbuf(s4, "at_qA", [128, S], BF16)
            qB = mk_sbuf(s4, "at_qB", [64, S], BF16)
            vSb = mk_sbuf(s4, "at_v", [128, S], BF16)
            Pb = mk_sbuf(s4, "at_P", [128, S], BF16)
            sq5 = mk_sbuf(s4, "at_sq", [128, 512], BF16)
            ptb = [mk_sbuf(s4, f"at_pt{i}", [128, 128], BF16) for i in range(2)]
            osg = mk_sbuf(s4, "at_osg", [128, 128], BF16)
            nqrow = mk_sbuf(s4, "at_nqrow", [1, S], F32)
            mrow = nqrow
            r512 = mk_sbuf(s4, "at_r512", [1, 512], F32)
            kmx = mk_sbuf(s4, "at_kmx", [1, 2], F32)
            mcol = mk_sbuf(s4, "at_mcol", [128, 2], F32)
            lcol = mk_sbuf(s4, "at_lcol", [128, 3], F32)

            CPQK = 4 * 192 * T
            CQQK = 1024 - HB * 192 * T
            CPV = 4 * 128
            CQV = 1024 * D - 16 * 128

            for prl in range(4):
                p.dma(kloc.rearrange("d (m c) -> d m c", m=8),
                      lambda e, c0=prl * 192 * T: bass.AP(
                          kag_out.tensor, p.emit_off(e, "POOL", c0, cp=CPQK, cq=CQQK),
                          [[T, 192], [HB * 192 * T, 8], [1, 1024]]),
                      reads=["kag_out"], writes=["kloc"], eng="POOL")
                p.dma(qloc.rearrange("d (m c) -> d m c", m=8),
                      lambda e, c0=prl * 192 * T: bass.AP(
                          qag_out.tensor, p.emit_off(e, "POOL", c0, cp=CPQK, cq=CQQK),
                          [[T, 192], [HB * 192 * T, 8], [1, 1024]]),
                      reads=["qag_out"], writes=["qloc"], eng="POOL")
                p.dma(lambda e: bass.AP(vloc.tensor, 0,
                                        [[128, 1024], [1024 * 128, 8], [1, 128]]),
                      lambda e, c0=prl * 128: bass.AP(
                          vag_out.tensor, p.emit_off(e, "POOL", c0, cp=CPV, cq=CQV),
                          [[D, 1024], [T * D, 8], [1, 128]]),
                      reads=["vag_out"], writes=["vloc"], eng="POOL")
                p.dma(kA[:, :], kloc[0:128, :], reads=["kloc"], writes=["at_kA"])
                p.dma(kB[:, :], kloc[128:192, :], reads=["kloc"], writes=["at_kB"])
                p.dma(qA[:, :], qloc[0:128, :], reads=["qloc"], writes=["at_qA"])
                p.dma(qB[:, :], qloc[128:192, :], reads=["qloc"], writes=["at_qB"])
                p.dma(vSb[:, :],
                      bass.AP(vloc.tensor, 0, [[128, 128], [128 * 128, 64], [1, 128]]),
                      reads=["vloc"], writes=["at_v"])

                for ch in range(16):
                    p.op("ACT", lambda s, ch=ch, sq=sq5, g=kA: s.activation(
                        sq[:, :], g[:, ch * 512:(ch + 1) * 512], AF.Square),
                        reads=["at_kA"], writes=["at_sq"])
                    p.op("PE", lambda t, sq=sq5: t.matmul(
                        PS[3][0:1, :], onescolB[:, :], sq[:, :],
                        start=True, stop=False), reads=["at_sq", "onescolB"], writes=["ps3"])
                    p.op("ACT", lambda s, ch=ch, sq=sq5, g=kB: s.activation(
                        sq[0:64, :], g[:, ch * 512:(ch + 1) * 512], AF.Square),
                        reads=["at_kB", "at_sq"], writes=["at_sq"])
                    p.op("PE", lambda t, sq=sq5: t.matmul(
                        PS[3][0:1, :], onescolB[0:64, :], sq[0:64, :],
                        start=False, stop=True), reads=["at_sq", "onescolB"], writes=["ps3"])
                    p.op("DVE", lambda v, r5=r512: v.tensor_copy(
                        r5[:, :], PS[3][0:1, :]), reads=["ps3"], writes=["at_r512"])
                    p.op("DVE", lambda v, r5=r512, ch=ch: v.reduce_max(
                        kmx[:, 1:2], r5[:, :], axis=AX.X), reads=["at_r512"], writes=["at_kmx"])
                    if ch == 0:
                        p.op("DVE", lambda v: v.tensor_copy(kmx[:, 0:1], kmx[:, 1:2]),
                             reads=["at_kmx"], writes=["at_kmx"])
                    else:
                        p.op("DVE", lambda v: v.tensor_tensor(
                            kmx[:, 0:1], kmx[:, 0:1], kmx[:, 1:2], ALU.max),
                            reads=["at_kmx"], writes=["at_kmx"])
                for ch in range(16):
                    p.op("ACT", lambda s, ch=ch, sq=sq5, g=qA: s.activation(
                        sq[:, :], g[:, ch * 512:(ch + 1) * 512], AF.Square),
                        reads=["at_qA"], writes=["at_sq"])
                    p.op("PE", lambda t, sq=sq5: t.matmul(
                        PS[3][0:1, :], onescolB[:, :], sq[:, :],
                        start=True, stop=False), reads=["at_sq", "onescolB"], writes=["ps3"])
                    p.op("ACT", lambda s, ch=ch, sq=sq5, g=qB: s.activation(
                        sq[0:64, :], g[:, ch * 512:(ch + 1) * 512], AF.Square),
                        reads=["at_qB", "at_sq"], writes=["at_sq"])
                    p.op("PE", lambda t, sq=sq5: t.matmul(
                        PS[3][0:1, :], onescolB[0:64, :], sq[0:64, :],
                        start=False, stop=True), reads=["at_sq", "onescolB"], writes=["ps3"])
                    p.op("DVE", lambda v, ch=ch, dr=nqrow: v.tensor_copy(
                        dr[:, ch * 512:(ch + 1) * 512], PS[3][0:1, :]),
                        reads=["ps3"], writes=["at_nrow"])
                p.op("DVE", lambda v: v.tensor_scalar(
                    mrow[:, :], nqrow[:, :], kmx[:, 0:1], 1.0 / 192.0, ALU.mult, ALU.mult),
                    reads=["at_nrow", "at_kmx"], writes=["at_nrow"])
                p.op("ACT", lambda s: s.activation(mrow[:, :], mrow[:, :], AF.Sqrt),
                     reads=["at_nrow"], writes=["at_nrow"])
                p.op("DVE", lambda v: v.tensor_scalar_mul(mrow[:, :], mrow[:, :], -1.0),
                     reads=["at_nrow"], writes=["at_nrow"])

                for qt in range(PT):
                    mq, aq = qt % 8, qt // 8
                    qcol = mq * 1024 + aq * 128
                    p.op("PE", lambda t, qcol=qcol: t.transpose(
                        PS[2][:, 0:1], mrow[:, qcol:qcol + 128], identF[0:1, 0:1]),
                        reads=["at_nrow", "identF"], writes=["ps2"])
                    p.op("DVE", lambda v: v.tensor_copy(mcol[:, 0:1], PS[2][:, 0:1]),
                         reads=["ps2"], writes=["at_mcol"])
                    runs = []
                    for mp in range(8):
                        full = (qt - mp + 7) // 8
                        c0 = mp * 1024
                        while full > 0:
                            w = min(full * 128, 512)
                            runs.append((c0, w, False))
                            c0 += w
                            full -= w // 128
                    runs.append((mq * 1024 + aq * 128, 128, True))
                    first_l = True
                    for (c0, w, is_diag) in runs:
                        bank = PS[4 + (len(runs) % 2) if False else 4 + (c0 // 512) % 2]
                        rb = f"ps{4 + (c0 // 512) % 2}"
                        fns = [
                            lambda t, c0=c0, w=w, qcol=qcol, bank=bank: t.matmul(
                                bank[:, 0:w], qA[:, qcol:qcol + 128], kA[:, c0:c0 + w],
                                start=True, stop=False),
                            lambda t, c0=c0, w=w, qcol=qcol, bank=bank: t.matmul(
                                bank[:, 0:w], qB[:, qcol:qcol + 128], kB[:, c0:c0 + w],
                                start=False, stop=True),
                        ]
                        p.group("PE", fns, reads=["at_qA", "at_qB", "at_kA", "at_kB"],
                                writes=[rb])
                        acc_ix = 2 if is_diag else 1
                        p.op("ACT", lambda s, c0=c0, w=w, bank=bank, ai=acc_ix: s.activation(
                            Pb[:, c0:c0 + w], bank[:, 0:w], AF.Exp,
                            bias=mcol[:, 0:1], scale=ATTN_SCALE,
                            accum_out=lcol[:, ai:ai + 1]),
                            reads=[rb, "at_mcol"], writes=["at_P", "at_lc"])
                        if is_diag:
                            p.op("DVE", lambda v, c0=c0: v.tensor_mul(
                                Pb[:, c0:c0 + 128], Pb[:, c0:c0 + 128], trilB[:, :]),
                                reads=["at_P", "tril"], writes=["at_P"])
                            p.op("DVE", lambda v, c0=c0: v.reduce_sum(
                                lcol[:, 2:3], Pb[:, c0:c0 + 128], axis=AX.X),
                                reads=["at_P"], writes=["at_lc"])
                            if first_l:
                                p.op("DVE", lambda v: v.tensor_copy(
                                    lcol[:, 0:1], lcol[:, 2:3]),
                                    reads=["at_lc"], writes=["at_lc"])
                                first_l = False
                            else:
                                p.op("DVE", lambda v: v.tensor_add(
                                    lcol[:, 0:1], lcol[:, 0:1], lcol[:, 2:3]),
                                    reads=["at_lc"], writes=["at_lc"])
                        elif first_l:
                            p.op("DVE", lambda v: v.tensor_copy(lcol[:, 0:1], lcol[:, 1:2]),
                                 reads=["at_lc"], writes=["at_lc"])
                            first_l = False
                        else:
                            p.op("DVE", lambda v: v.tensor_add(
                                lcol[:, 0:1], lcol[:, 0:1], lcol[:, 1:2]),
                                reads=["at_lc"], writes=["at_lc"])
                    p.op("DVE", lambda v: v.reciprocal(lcol[:, 1:2], lcol[:, 0:1]),
                         reads=["at_lc"], writes=["at_lc"])
                    for (c0, w, is_diag) in runs:
                        p.op("DVE", lambda v, c0=c0, w=w: v.tensor_scalar_mul(
                            Pb[:, c0:c0 + w], Pb[:, c0:c0 + w], lcol[:, 1:2]),
                            reads=["at_P", "at_lc"], writes=["at_P"])
                    taus = []
                    for (c0, w, is_diag) in runs:
                        for cc in range(c0, c0 + w, 128):
                            taus.append(cc // 128)
                    for ti_, tau in enumerate(taus):
                        bank = PB[ti_ % 2]
                        rpb = f"pb{ti_ % 2}"
                        p.op("PE", lambda t, tau=tau, bank=bank: t.transpose(
                            bank[:, 0:128], Pb[:, tau * 128:(tau + 1) * 128], identB[:, :]),
                            reads=["at_P", "identB"], writes=[rpb])
                        p.op("DVE", lambda v, ti_=ti_, bank=bank, pp=ptb[ti_ % 2]: v.tensor_copy(
                            pp[:, :], bank[:, 0:128]), reads=[rpb], writes=[f"at_ptb{ti_ % 2}"])
                        p.op("PE", lambda t, tau=tau, ti_=ti_, pp=ptb[ti_ % 2], n=len(taus): t.matmul(
                            PS[0][:, 0:128], vSb[:, tau * 128:(tau + 1) * 128], pp[:, :],
                            start=(ti_ == 0), stop=(ti_ == n - 1)),
                            reads=[f"at_ptb{ti_ % 2}", "at_v"], writes=["ps0"])
                    p.op("ACT", lambda s: s.copy(osg[:, :], PS[0][:, 0:128]),
                         reads=["ps0"], writes=["at_osg"])
                    p.dma(oag_in[prl * 128:(prl + 1) * 128, qt * 128:(qt + 1) * 128],
                          osg[:, :], reads=["at_osg"], writes=[f"oag{prl}_{qt}"])
            p.barrier()

        p.collective("AllGather", ALU.bypass, [oag_in], [oag_out],
                     reads=[f"oag{prl}_{qt}" for prl in range(4) for qt in range(PT)],
                     writes=["oag_out"])

        for b2 in range(2):
            p.dma(oloc.rearrange("d (k i) -> d k i", k=16)[:, 8 * b2:8 * b2 + 8, :],
                  lambda e, b2=b2: bass.AP(
                      oag_out.tensor, p.emit_off(e, "SP", b2 * 2048 * S, cp=128),
                      [[S, 2048], [1024, 8], [1, 128]]),
                  reads=["oag_out"], writes=["oloc"], eng="SP")

        stW2 = stack.enter_context(ExitStack())
        cb2 = mk_sbuf(stW2, "cb2", [128, ND * T], BF16)
        with ExitStack() as s5:
            TH = T // 2
            oT_sb = mk_sbuf(s5, "oT_sb", [128, ND * TH], BF16)
            wpb = [mk_sbuf(s5, f"wpb{i}", [128, 16 * 128], BF16) for i in range(2)]
            for th in range(2):
                for k in range(ND):
                    p.dma(oT_sb[:, k * TH:(k + 1) * TH],
                          oloc[k * 128:(k + 1) * 128, th * TH:(th + 1) * TH],
                          reads=["oloc"], writes=[f"oTs{k}"])
                for o in range(ND):
                    wi = o % 2
                    p.dma(wpb[wi][:, :],
                          I["bWproj"].rearrange("(k p) o -> p k o", p=128)[:, :, o * 128:(o + 1) * 128],
                          writes=[f"wpb{wi}"], eng="POOL")
                    for tc in range(2):
                        bank = PS[4 + (tc % 2)]
                        rb = f"ps{4 + (tc % 2)}"
                        fns = []
                        for k in range(ND):
                            fns.append(lambda t, k=k, tc=tc, wb=wpb[wi], bank=bank, ot=oT_sb: t.matmul(
                                bank[:, :], wb[:, k * 128:(k + 1) * 128],
                                ot[:, k * TH + tc * 512:k * TH + (tc + 1) * 512],
                                start=(k == 0), stop=(k == ND - 1)))
                        p.group("PE", fns,
                                reads=[f"wpb{wi}"] + [f"oTs{k}" for k in range(ND)],
                                writes=[rb])
                        p.op("ACT", lambda s, tc=tc, o=o, th=th, bank=bank, cb=cb2: s.copy(
                            cb[:, o * T + th * TH + tc * 512:o * T + th * TH + (tc + 1) * 512],
                            bank[:, :]),
                            reads=[rb], writes=[f"cb2_{o}"])
            p.barrier()

        with ExitStack() as st3:
            emit_boundary("b3", xr2, cb2, "cb2_", xr1, "xr1", st3, do_ln=True,
                          src_region="xr2")
            p.barrier()
        stW2.close()

        with ExitStack() as st4:
            emit_ffn("f2", xr1, "xr1", out, "out", st4, do_ln=False)

        with ExitStack() as stZ:
            if debug_dump is not None:
                for k in range(ND):
                    p.dma(dbg[k * 128:(k + 1) * 128, :], yTt(k), reads=[f"yT{k}"],
                          writes=[f"dbg{k}"], eng="POOL")
            p.barrier()
            p.drain_dmas(eng="SP")
        p.finalize()

    return nc


def kernel(**inputs):
    in_maps = host_inputs(inputs)
    nc = build_nc()
    res = run_bass_kernel_spmd(nc, in_maps, core_ids=list(range(NCORE)))
    o4 = np.zeros((B, PT, 128, D), dtype=np.float32)
    for c in range(NCORE):
        shard = np.asarray(res.results[c]["out_shard"], dtype=np.float32).reshape(NT, 128, D)
        tiles = owner_tiles(c)
        i = 0
        for b in range(B):
            for pt in tiles:
                o4[b, pt] = shard[i]
                i += 1
    return o4.reshape(B, S, D)


